# version 20
# baseline (speedup 1.0000x reference)
import numpy as np
import ml_dtypes
from contextlib import ExitStack
import concourse.bass as bass
import concourse.mybir as mybir
from concourse.bass_utils import run_bass_kernel_spmd

F32 = mybir.dt.float32
BF16 = mybir.dt.bfloat16
U8 = mybir.dt.uint8
AF = mybir.ActivationFunctionType
OP = mybir.AluOpType

H, D, DH, C, T = 8, 1024, 128, 64, 2048
NCH = 32
TT = 512
NTT = T // TT
XBS = 2080

ENGS = ("sync", "gpsimd", "tensor", "vector", "scalar")
LAST = "LAST"


class Plan:

    def __init__(self, sems):
        self.prog = {e: [] for e in ENGS}
        self.sems = sems
        self.cnt = {k: 0 for k in sems}
        self.ids = {}
        self.waited = {e: {k: 0 for k in sems} for e in ENGS}

    def step(self, eng, fn, wait=(), inc=None, inc_n=1, self_inc=False,
             sid=None):
        bysem = {}
        for w in wait:
            if w[0] == "ID":
                if w[1] is None:
                    continue
                s, v = self.ids[w[1]]
            else:
                s, v = w
                if v == LAST:
                    v = self.cnt[s]
            bysem[s] = max(bysem.get(s, 0), v)
        todo = []
        for s, v in bysem.items():
            assert v <= self.cnt[s], f"wait on future {s}: {v} > {self.cnt[s]}"
            if v > self.waited[eng][s]:
                todo.append((s, v))
                self.waited[eng][s] = v
        sems = self.sems
        if inc is not None:
            self.cnt[inc] += inc_n
            if sid is not None:
                self.ids[sid] = (inc, self.cnt[inc])

        def run(q, todo=todo, fn=fn, inc=inc, inc_n=inc_n, self_inc=self_inc):
            for s, v in todo:
                q.wait_ge(sems[s], v)
            last = fn(q)
            if inc is not None and not self_inc:
                assert last is not None
                last.then_inc(sems[inc], inc_n)
        self.prog[eng].append(run)
        return self.cnt[inc] if inc is not None else None

    def barrier(self, k):
        sems = self.sems
        tgt = 5 * k
        self.cnt["B"] += 5
        assert self.cnt["B"] == tgt
        for e in ENGS:
            def f(q, tgt=tgt):
                q.sem_inc(sems["B"], 1)
                q.wait_ge(sems["B"], tgt)
            self.prog[e].append(f)
            self.waited[e]["B"] = tgt


def build_nc(debug_dumps=False, phase_limit=4):
    nc = bass.Bass("TRN2", target_bir_lowering=False, debug=False,
                   detect_race_conditions=False)

    x_t = nc.dram_tensor("x_t", [D, T], F32, kind="ExternalInput").ap()
    wwT = nc.dram_tensor("wwT", [D, D], F32, kind="ExternalInput").ap()
    woT = nc.dram_tensor("woT", [D, D], F32, kind="ExternalInput").ap()
    wgbaT = nc.dram_tensor("wgbaT", [D, 104], F32, kind="ExternalInput").ap()
    dtb_h = nc.dram_tensor("dtb", [8, 1], F32, kind="ExternalInput").ap()
    alog_h = nc.dram_tensor("alog", [8, 1], F32, kind="ExternalInput").ap()
    k0c_h = nc.dram_tensor("k0c", [32, 128, 2048], U8, kind="ExternalInput").ap()
    k1_h = nc.dram_tensor("k1", [32, 128, 4096], U8, kind="ExternalInput").ap()
    k2_h = nc.dram_tensor("k2", [8, 128, 4096], U8, kind="ExternalInput").ap()
    s0_h = nc.dram_tensor("s0", [1, 1], F32, kind="ExternalInput").ap()
    s1_h = nc.dram_tensor("s1", [1, 1], F32, kind="ExternalInput").ap()
    s2_h = nc.dram_tensor("s2", [1, 1], F32, kind="ExternalInput").ap()
    y_t = nc.dram_tensor("y_t", [D, T], F32, kind="ExternalOutput").ap()
    x1dram = nc.dram_tensor("x1dram", [D, T], F32, kind="Internal").ap()
    dumps = {}
    if debug_dumps:
        for nm, shape, dt in [("d_rowsp", [128, XBS], F32),
                              ("d_vti", [128, T], F32),
                              ("d_of", [128, H * T], BF16),
                              ("d_x1", [128, H * T], BF16),
                              ("d_vp", [128, T], BF16),
                              ("d_att", [128, T], BF16),
                              ("d_wkcd", [128, T], BF16),
                              ("d_edl", [128, 256], F32)]:
            dumps[nm] = nc.dram_tensor(nm, shape, dt, kind="ExternalOutput").ap()

    bf = ml_dtypes.bfloat16
    ident_c = nc.inline_tensor(np.eye(128, dtype=bf), name="ident_c").ap()
    idf32_c = nc.inline_tensor(np.eye(128, dtype=np.float32), name="idf32_c").ap()
    jj = np.arange(64)
    blkS = np.where(jj[None, :] > jj[:, None], 0.0, -1e30).astype(np.float32)
    blkI = np.where(jj[None, :] >= jj[:, None], 0.0, -1e30).astype(np.float32)
    mS = np.full((128, 128), -1e30, np.float32)
    mI = np.full((128, 128), -1e30, np.float32)
    for b0 in (0, 64):
        mS[b0:b0+64, b0:b0+64] = blkS
        mI[b0:b0+64, b0:b0+64] = blkI
    masks_c = nc.inline_tensor(mS, name="masks_c").ap()
    maski_c = nc.inline_tensor(mI, name="maski_c").ap()
    selc_np = np.zeros((128, 64), dtype=bf)
    for h in range(8):
        selc_np[:, h*8 + h] = bf(1.0)
    selc_c = nc.inline_tensor(selc_np, name="selc_c").ap()
    selr_np = np.zeros((128, 1024), np.float32)
    selr16_np = np.zeros((128, 1024), dtype=bf)
    for h in range(8):
        selr_np[32 + h, h*128:(h+1)*128] = 1.0
        selr16_np[64 + h, h*128:(h+1)*128] = bf(1.0)
    selr_c = nc.inline_tensor(selr_np, name="selr_c").ap()
    selr16_c = nc.inline_tensor(selr16_np, name="selr16_c").ap()
    meanc_c = nc.inline_tensor(np.full((128, 1), 1.0/1024, dtype=bf), name="meanc_c").ap()
    halfn_c = nc.inline_tensor(np.full((128, 1), -0.5, dtype=bf), name="halfn_c").ap()
    one128_c = nc.inline_tensor(np.ones((1, 128), np.float32), name="one128_c").ap()
    comb_np = np.zeros((4, 128, 128), np.float32)
    for h in range(8):
        comb_np[0, 0 + h, 64 + h] = 1.0
        comb_np[0, 32 + h, 0 + h] = 1.0
        comb_np[0, 96 + h, 32 + h] = 1.0
        comb_np[1, 32 + h, 64 + h] = 1.0
        comb_np[1, 32 + h, 32 + h] = 1.0
        comb_np[1, 32 + h, 96 + h] = -1.0
        comb_np[2, 0 + h, 64 + h] = 1.0
        comb_np[3, 0 + h, 32 + h] = 1.0
        comb_np[3, 0 + h, 0 + h] = 1.0
        comb_np[3, 0 + h, 96 + h] = 1.0
    comb_c = [nc.inline_tensor(comb_np[i], name=f"comb{i}_c").ap()
              for i in range(4)]
    onesr_c = nc.inline_tensor(np.ones((1, T), np.float32), name="onesr_c").ap()

    sem_names = ["L", "G", "H", "I", "J", "K", "T", "V", "A", "B", "O", "P"]
    es0 = ExitStack()
    sems = {k: es0.enter_context(nc.semaphore("s" + k)) for k in sem_names}
    P = Plan(sems)

    sb = lambda st, name, shape, dt: st.enter_context(nc.sbuf_tensor(name, shape, dt))
    IDENT = sb(es0, "IDENT", [128, 128], BF16)
    IDF32 = sb(es0, "IDF32", [128, 128], F32)
    MASKS = sb(es0, "MASKS", [128, 128], F32)
    MASKI = sb(es0, "MASKI", [128, 128], F32)
    SELC = sb(es0, "SELC", [128, 64], BF16)
    SELR = sb(es0, "SELR", [128, 1024], F32)
    SELR16 = sb(es0, "SELR16", [128, 1024], BF16)
    MEANC = sb(es0, "MEANC", [128, 1], BF16)
    HALFN = sb(es0, "HALFN", [128, 1], BF16)
    ONE128 = sb(es0, "ONE128", [1, 128], F32)
    DTB = sb(es0, "DTB", [128, 1], F32)
    NEALOG = sb(es0, "NEALOG", [128, 1], F32)
    COMB = [sb(es0, f"COMB{i}", [128, 128], F32) for i in range(4)]
    SCL = sb(es0, "SCL", [128, 4], F32)

    es123 = ExitStack()
    WT = sb(es123, "WT", [128, 8 * 1024], BF16)
    OF = sb(es123, "OF", [128, H * T], BF16)

    es12 = ExitStack()
    XB = sb(es12, "XB", [128, 8 * XBS], BF16)
    WGBA = sb(es12, "WGBA", [128, 8 * 128], BF16)
    ROWSP = sb(es12, "ROWSP", [128, XBS], F32)
    RB16 = sb(es12, "RB16", [128, T], BF16)
    VT = sb(es12, "VT", [128, 16 * 128], F32)
    EDL = sb(es12, "EDL", [128, 256], F32)
    VF = [sb(es12, f"VF{i}", [128, T], BF16) for i in range(2)]
    VP = [sb(es12, f"VP{i}", [128, T], BF16) for i in range(2)]
    WKCD = [sb(es12, f"WKCD{i}", [128, T], BF16) for i in range(2)]
    ATT = [sb(es12, f"ATT{i}", [128, T], BF16) for i in range(2)]
    WKDW = [sb(es12, f"WKDW{i}", [128, T], BF16) for i in range(2)]
    RKD = [sb(es12, f"RKD{i}", [128, T], BF16) for i in range(2)]
    PK = [sb(es12, f"PK{i}", [128, 128], BF16) for i in range(2)]
    PK2 = [sb(es12, f"PK2{i}", [128, 128], BF16) for i in range(2)]
    NPA = [sb(es12, f"NPA{i}", [128, 256], BF16) for i in range(2)]
    NPB = [sb(es12, f"NPB{i}", [128, 256], BF16) for i in range(2)]
    EMS = [sb(es12, f"EMS{i}", [128, 128], F32) for i in range(2)]
    BT = [sb(es12, f"BT{i}", [128, 128], BF16) for i in range(2)]
    WBDT = [sb(es12, f"WBDT{i}", [128, 128], BF16) for i in range(2)]
    SF32 = sb(es12, "SF32", [128, H * 128], F32)
    SB16 = sb(es12, "SB16", [128, H * 128], BF16)
    VNEW = sb(es12, "VNEW", [128, 128], BF16)
    es12a = ExitStack()
    ROWST1 = sb(es12a, "ROWST1", [128, T], F32)
    ROWST2 = sb(es12a, "ROWST2", [128, T], F32)
    VTI = sb(es12a, "VTI", [128, T], F32)
    XSQ = sb(es12a, "XSQ", [128, TT], BF16)

    ps12 = ExitStack()
    psm = lambda st, name, shape, dt=F32: st.enter_context(nc.psum_tensor(name, shape, dt))
    PBIG = psm(ps12, "PBIG", [128, 512])
    PSA = [psm(ps12, f"PSA{i}", [128, 512]) for i in range(2)]
    PSB = [psm(ps12, f"PSB{i}", [128, 512]) for i in range(2)]
    PTB = [psm(ps12, f"PTB{i}", [128, 512], BF16) for i in range(2)]
    PSD = psm(ps12, "PSD", [128, 512])

    def dmaL(q, dst, src):
        return q.dma_start(out=dst, in_=src).then_inc(sems["L"], 16)

    def dmaG(q, dst, src):
        return q.dma_start(out=dst, in_=src).then_inc(sems["G"], 16)

    WSEM = ["H", "I", "J", "K"]

    def dmaW(q, dst, src, buf):
        return q.dma_start(out=dst, in_=src).then_inc(sems[WSEM[buf]], 16)

    def dmaO(q, dst, src):
        return q.dma_start(out=dst, in_=src).then_inc(sems["O"], 16)

    def c_loads(q):
        dmaL(q, IDENT[:], ident_c)
        dmaL(q, IDF32[:], idf32_c)
        dmaL(q, MASKS[:], masks_c)
        dmaL(q, MASKI[:], maski_c)
        dmaL(q, SELC[:], selc_c)
        dmaL(q, SELR[:], selr_c)
        dmaL(q, SELR16[:], selr16_c)
        dmaL(q, MEANC[:], meanc_c)
        dmaL(q, HALFN[:], halfn_c)
        dmaL(q, ONE128[:], one128_c)
        for i in range(4):
            dmaL(q, COMB[i][:], comb_c[i])
        dmaL(q, DTB[32:40, :], dtb_h)
        return dmaL(q, NEALOG[32:40, :], alog_h)
    if phase_limit >= 0.03:
        nL0 = P.step("sync", c_loads, inc="L", inc_n=16 * 16, self_inc=True)
    else:
        nL0 = 0

    def g_loads(q):
        last = None
        for h in range(8):
            last = dmaG(q, XB[:, h*XBS+32 : h*XBS+32+T], x_t[h*128:(h+1)*128, :])
        for kc in range(8):
            last = dmaG(q, WT[:, kc*1024:(kc+1)*1024], wwT[kc*128:(kc+1)*128, :])
        for kc in range(8):
            last = dmaG(q, WGBA[:, kc*128:kc*128+104], wgbaT[kc*128:(kc+1)*128, :])
        return last
    if phase_limit >= 0.12:
        nG0 = P.step("gpsimd", g_loads, inc="G", inc_n=16 * 24, self_inc=True)
    else:
        nG0 = 0

    def v_init(q):
        for h in range(8):
            q.memset(XB[:, h*XBS : h*XBS+32], 0.0)
        q.memset(ROWSP[:], 0.0)
        q.memset(ROWST1[:], 0.0)
        q.memset(ROWST2[:], 0.0)
        q.memset(SF32[:], 0.0)
        q.memset(VTI[:], 0.0)
        q.memset(PSA[0][:], 0.0)
        q.memset(PSA[1][:], 0.0)
        q.memset(PSD[:], 0.0)
        return q.memset(SB16[:], 0.0)
    if phase_limit >= 0.15:
        P.step("vector", v_init, wait=[("L", nL0), ("G", nG0)], inc="V")
    nV_init = P.cnt["V"]

    def a_nealog(q):
        q.activation(NEALOG[32:40, :], NEALOG[32:40, :], AF.Exp)
        return q.activation(NEALOG[32:40, :], NEALOG[32:40, :], AF.Copy,
                            scale=-1.0)
    nA_neal = (P.step("scalar", a_nealog, wait=[("L", nL0)], inc="A")
               if phase_limit >= 0.17 else 0)

    for tt in (range(NTT) if phase_limit >= 0.2 else []):
        def t_gba(q, tt=tt):
            last = None
            for kc in range(8):
                last = q.matmul(PSA[0][0:104, :], WGBA[:, kc*128:kc*128+104],
                                XB[:, kc*XBS+32+tt*TT : kc*XBS+32+tt*TT+TT],
                                start=(kc == 0), stop=(kc == 7))
            return last
        nT = P.step("tensor", t_gba,
                    wait=[("L", nL0), ("G", nG0), ("V", LAST)], inc="T")
        def v_gba(q, tt=tt):
            q.tensor_copy(ROWST1[0:8, tt*TT:(tt+1)*TT], PSA[0][0:8, :])
            q.tensor_copy(ROWST1[32:40, tt*TT:(tt+1)*TT], PSA[0][32:40, :])
            return q.tensor_copy(ROWST1[96:104, tt*TT:(tt+1)*TT], PSA[0][96:104, :])
        P.step("vector", v_gba, wait=[("T", nT)], inc="V")

    for tt in (range(NTT) if phase_limit >= 0.3 else []):
        for kc in range(8):
            def v_sq1(q, tt=tt, kc=kc):
                sl = XB[:, kc*XBS+32+tt*TT : kc*XBS+32+tt*TT+TT]
                return q.tensor_tensor(XSQ[:], sl, sl, OP.mult)
            nV = P.step("vector", v_sq1, wait=[("T", LAST)], inc="V")
            def t_ns(q, tt=tt, kc=kc):
                return q.matmul(PSA[1][0:8, :], SELC[:, kc*8:(kc+1)*8], XSQ[:],
                                start=(kc == 0), stop=(kc == 7))
            P.step("tensor", t_ns, wait=[("V", nV)], inc="T")
        def v_ns(q, tt=tt):
            return q.tensor_copy(ROWST2[0:8, tt*TT:(tt+1)*TT], PSA[1][0:8, :])
        P.step("vector", v_ns, wait=[("T", LAST)], inc="V")

    GL = ROWST1[0:8, 0:T]
    ZR = ROWST1[32:40, 0:T]
    BL = ROWST1[96:104, 0:T]
    NS2 = ROWST2[0:8, 0:T]
    DECS = ROWST2[32:40, 0:T]
    SCB = ROWST2[96:104, 0:T]
    LNNIV = ROWSP[0:8, 1:T+1]
    LNNSH = ROWSP[0:8, 0:T]
    T2G = ROWSP[0:8, 0:T]
    T2A = ROWSP[32:40, 0:T]
    T2B = ROWSP[96:104, 0:T]
    DWNIV = VTI[0:8, 0:T]
    T1G = VTI[0:8, 0:T]
    RA = VTI[32:40, 0:T]
    BETA = VTI[96:104, 0:T]
    EBA = RB16[64:72, 0:T]

    def a_rows1(q):
        q.activation(T1G, GL, AF.Abs)
        q.activation(T1G, T1G, AF.Exp, scale=-1.0)
        q.activation(T2G, T1G, AF.Ln, bias=1.0)
        return q.activation(T1G, GL, AF.Relu, scale=-1.0)
    nA = P.step("scalar", a_rows1, wait=[("V", LAST)], inc="A")
    def v_rows1(q):
        return q.scalar_tensor_tensor(GL, T1G, -1.0, T2G, OP.mult, OP.subtract)
    nV = P.step("vector", v_rows1, wait=[("A", nA)], inc="V")
    def a_rows2(q):
        q.activation(BETA, BL, AF.Sigmoid)
        q.activation(SCB, BL, AF.Abs)
        q.activation(SCB, SCB, AF.Exp, scale=-1.0)
        q.activation(T2B, SCB, AF.Ln, bias=1.0)
        return q.activation(SCB, BL, AF.Relu, scale=-1.0)
    nA = P.step("scalar", a_rows2, wait=[("V", nV)], inc="A")
    def v_rows2(q):
        return q.scalar_tensor_tensor(BL, SCB, -1.0, T2B, OP.mult, OP.subtract)
    nV = P.step("vector", v_rows2, wait=[("A", nA)], inc="V")
    def a_rows3(q):
        q.activation(DECS, ZR, AF.Abs, bias=DTB[32:40, :])
        q.activation(DECS, DECS, AF.Exp, scale=-1.0)
        q.activation(T2A, DECS, AF.Ln, bias=1.0)
        return q.activation(RA, ZR, AF.Relu, bias=DTB[32:40, :])
    nA = P.step("scalar", a_rows3, wait=[("V", nV)], inc="A")
    def v_rows3(q):
        q.tensor_add(ZR, RA, T2A)
        return q.tensor_scalar(DECS, ZR, NEALOG[32:40, :], None, OP.mult)
    nV = P.step("vector", v_rows3, wait=[("A", nA), ("A", nA_neal)], inc="V")

    src_is_dec = True
    for s in [1, 2, 4, 8, 16, 32]:
        def v_cs(q, s=s, src_is_dec=src_is_dec):
            sr = DECS if src_is_dec else RA
            ds = RA if src_is_dec else DECS
            s3 = sr.rearrange("p (n c) -> p n c", c=64)
            d3 = ds.rearrange("p (n c) -> p n c", c=64)
            q.tensor_copy(d3[:, :, 0:s], s3[:, :, 0:s])
            return q.tensor_add(d3[:, :, s:64], s3[:, :, s:64], s3[:, :, 0:64-s])
        nV = P.step("vector", v_cs, inc="V")
        src_is_dec = not src_is_dec
    assert src_is_dec

    def v_ln1(q):
        return q.tensor_scalar_max(LNNIV, NS2, 1e-24)
    nV = P.step("vector", v_ln1, inc="V")
    def a_ln(q):
        return q.activation(LNNIV, LNNIV, AF.Ln)
    nA = P.step("scalar", a_ln, wait=[("V", nV)], inc="A")
    def v_ln2(q):
        q.tensor_scalar_mul(LNNIV, LNNIV, -0.5)
        q.memset(ROWSP[0:8, 0:1], 0.0)
        dec3 = DECS.rearrange("p (n c) -> p n c", c=64)
        db3 = ZR.rearrange("p (n c) -> p n c", c=64)
        last3 = dec3[:, :, 63:64].broadcast_to([8, NCH, 64])
        return q.tensor_sub(db3[:, :, :], last3, dec3[:, :, :])
    nV = P.step("vector", v_ln2, wait=[("A", nA)], inc="V")

    nA_exps = None
    for tt in range(NTT):
        def t_cmb(q, tt=tt):
            c0, c1 = tt * TT, (tt + 1) * TT
            q.matmul(PSA[0][:, :], COMB[0][:], ROWST1[:, c0:c1],
                     start=True, stop=False)
            q.matmul(PSA[0][:, :], COMB[1][:], ROWST2[:, c0:c1],
                     start=False, stop=False)
            q.matmul(PSA[0][:, :], COMB[2][:], ROWSP[:, c0+1:c1+1],
                     start=False, stop=False)
            return q.matmul(PSA[0][:, :], COMB[3][:], ROWSP[:, c0:c1],
                            start=False, stop=True)
        nT = P.step("tensor", t_cmb, wait=[("V", nV), ("V", LAST), ("A", LAST)],
                    inc="T")
        def v_cmb(q, tt=tt):
            c0, c1 = tt * TT, (tt + 1) * TT
            q.tensor_copy(ROWSP[64:72, c0:c1], PSA[0][64:72, :])
            q.tensor_copy(ROWSP[32:40, c0:c1], PSA[0][32:40, :])
            return q.tensor_copy(ROWSP[96:104, c0:c1], PSA[0][96:104, :])
        nVc = P.step("vector", v_cmb, wait=[("T", nT)], inc="V")
        def a_cmb(q, tt=tt):
            c0, c1 = tt * TT, (tt + 1) * TT
            q.activation(RB16[64:72, c0:c1], PSA[0][64:72, :], AF.Exp)
            q.activation(VTI[32:40, c0:c1], PSA[0][32:40, :], AF.Exp)
            return q.activation(VTI[0:8, c0:c1], PSA[0][0:8, :], AF.Exp)
        nA_exps = P.step("scalar", a_cmb, wait=[("T", nT), ("V", nVc)], inc="A")

    def t_edl(q):
        dec3 = DECS.rearrange("p (n c) -> p n c", c=64)
        rhs = dec3[:, :, 63:64].rearrange("p n c -> p (n c)")
        last = None
        for h in range(8):
            last = q.matmul(PBIG[:, h*32:(h+1)*32], SELR[32:40, h*128:(h+1)*128],
                            rhs, start=True, stop=True)
        return last
    nT = P.step("tensor", t_edl, wait=[("V", nV)], inc="T")
    def a_edl(q):
        return q.activation(EDL[:, 0:256], PBIG[:, 0:256], AF.Exp)
    nA_edl = P.step("scalar", a_edl, wait=[("T", nT)], inc="A")

    for p in range(16):
        def t_vt(q, p=p):
            return q.transpose(PSB[0][:, 0:128], VTI[:, p*128:(p+1)*128],
                               IDF32[:])
        nT = P.step("tensor", t_vt, wait=[("A", nA_exps), ("V", LAST)], inc="T")
        def v_vt(q, p=p):
            return q.tensor_copy(VT[:, p*128:(p+1)*128], PSB[0][:, 0:128])
        P.step("vector", v_vt, wait=[("T", nT)], inc="V")

    if debug_dumps:
        def s_dump_rows(q):
            dmaO(q, dumps["d_rowsp"], ROWSP[:])
            dmaO(q, dumps["d_vti"], VTI[:])
            return dmaO(q, dumps["d_edl"], EDL[:])
        P.step("sync", s_dump_rows,
               wait=[("V", LAST), ("A", LAST)], inc="O", inc_n=48, self_inc=True)

    es12a.close()
    es12b = ExitStack()
    RNKL = sb(es12b, "RNKL", [2, T], F32)
    RNKM = sb(es12b, "RNKM", [2, T], F32)
    RNKA = sb(es12b, "RNKA", [2, T], F32)
    def s_rnk_ones(q):
        dmaL(q, RNKL[1:2, :], onesr_c)
        dmaL(q, RNKM[0:1, :], onesr_c)
        return dmaL(q, RNKA[0:1, :], onesr_c)
    nL_ones = P.step("sync", s_rnk_ones,
                     wait=[("V", LAST), ("A", LAST), ("T", LAST)],
                     inc="L", inc_n=48, self_inc=True)

    P.ids["z"] = ("V", 0)
    pst = [dict(ptb="z", psa="z", psb_a=None, psb_v="z"),
           dict(ptb="z", psa="z", psb_a=None, psb_v="z")]
    P.ids["p1_vvt"] = ("V", P.cnt["V"])
    pst[0]["psb_v"] = "p1_vvt"
    headst = {"pbig": None, "tr2last": None, "as_prev": None}
    P.ids["p1_aedl"] = ("A", P.cnt["A"])
    headst["pbig_a"] = "p1_aedl"

    def plan_head_pairs(h, cur):
        steps = []
        def add(eng, fn, wait=(), inc=None, sid=None):
            steps.append(dict(eng=eng, fn=fn, wait=list(wait), inc=inc,
                              inc_n=1, self_inc=False, sid=sid))
        hb = h * XBS

        for tt in range(NTT):
            def t_vp(q, h=h, tt=tt):
                last = None
                for kc in range(8):
                    last = q.matmul(PBIG[:, :],
                                    WT[:, kc*1024+h*128 : kc*1024+(h+1)*128],
                                    XB[:, kc*XBS+32+tt*TT : kc*XBS+32+tt*TT+TT],
                                    start=(kc == 0), stop=(kc == 7))
                return last
            add("tensor", t_vp, wait=[("ID", headst["pbig"]),
                                      ("ID", headst["pbig_a"])],
                inc="T", sid=f"tvp{h}_{tt}")
            def v_vf(q, tt=tt, cur=cur):
                return q.tensor_copy(VF[cur][:, tt*TT:(tt+1)*TT], PBIG[:, :])
            add("vector", v_vf, wait=[("ID", f"tvp{h}_{tt}")], inc="V",
                sid=f"vvf{h}_{tt}")
            headst["pbig"] = f"vvf{h}_{tt}"

        for tt in range(NTT):
            def t_rk(q, h=h, tt=tt):
                return q.matmul(PBIG[:, :], SELR16[64:72, h*128:(h+1)*128],
                                RB16[64:72, tt*TT:(tt+1)*TT], start=True,
                                stop=True)
            add("tensor", t_rk, wait=[("ID", headst["pbig"])], inc="T",
                sid=f"trk{h}_{tt}")
            def v_rk(q, h=h, tt=tt, cur=cur, hb=hb):
                return q.tensor_tensor(RKD[cur][:, tt*TT:(tt+1)*TT],
                                       XB[:, hb+32+tt*TT : hb+32+tt*TT+TT],
                                       PBIG[:, :], OP.mult)
            add("vector", v_rk, wait=[("ID", f"trk{h}_{tt}")], inc="V",
                sid=f"vrk{h}_{tt}")
            headst["pbig"] = f"vrk{h}_{tt}"

        def g_rnk(q, h=h):
            dmaG(q, RNKL[0:1, 0:T], ROWSP[96+h:96+h+1, 0:T])
            dmaG(q, RNKM[1:2, 0:T], ROWSP[32+h:32+h+1, 0:T])
            return dmaG(q, RNKA[1:2, 0:T], ROWSP[64+h:96+h+1-64+64-h+h, 0:T] if False else ROWSP[64+h:64+h+1, 0:T])
        add("gpsimd", g_rnk, wait=[("L", nL_ones), ("ID", headst["tr2last"]),
                                   ("V", P.cnt["V"]), ("A", P.cnt["A"])],
            inc="G", sid=f"grnk{h}")
        steps[-1]["inc_n"] = 48
        steps[-1]["self_inc"] = True

        def plan_pair(h, p, sl):
            st = pst[sl]
            u = f"{h}_{p}"
            ss = []
            def padd(eng, fn, wait=(), inc=None, sid=None):
                ss.append(dict(eng=eng, fn=fn, wait=list(wait), inc=inc,
                               inc_n=1, self_inc=False, sid=sid))
            co = p * 128
            xs = XB[:, hb + 31 + co : hb + 31 + co + 128]
            xu = XB[:, hb + 32 + co : hb + 32 + co + 128]

            def t_tx(q, xs=xs, sl=sl):
                return q.transpose(PTB[sl][:, 0:128], xs, IDENT[:])
            padd("tensor", t_tx, wait=[("ID", st["ptb"])], inc="T",
                 sid=f"ttx{u}")
            def a_wk(q, p=p, h=h, sl=sl, cur=cur, co=co):
                q.activation(WKDW[cur][:, co:co+128], PTB[sl][:, 0:128],
                             AF.Copy, scale=VT[:, p*128+0+h : p*128+0+h+1])
                return q.activation(WBDT[sl][:], PTB[sl][:, 0:128], AF.Copy,
                                    scale=VT[:, p*128+32+h : p*128+32+h+1])
            padd("scalar", a_wk, wait=[("ID", f"ttx{u}")], inc="A",
                 sid=f"vwk{u}")
            def t_tv(q, sl=sl, cur=cur, co=co):
                return q.transpose(PTB[sl][:, 128:256], VF[cur][:, co:co+128],
                                   IDENT[:])
            padd("tensor", t_tv, wait=[("ID", f"vwk{u}"),
                                       ("ID", f"vvf{h}_{p//4}")], inc="T",
                 sid=f"ttv{u}")
            def a_vp(q, p=p, h=h, sl=sl, cur=cur, co=co):
                return q.activation(VP[cur][:, co:co+128], PTB[sl][:, 128:256],
                                    AF.Copy,
                                    scale=VT[:, p*128+96+h : p*128+96+h+1])
            padd("scalar", a_vp, wait=[("ID", f"ttv{u}")], inc="A",
                 sid=f"vvp{u}")

            def t_raw(q, xs=xs, xu=xu, sl=sl):
                for b0 in (0, 64):
                    q.matmul(PSA[sl][b0:b0+64, b0:b0+64], xs[:, b0:b0+64],
                             xs[:, b0:b0+64], start=True, stop=True)
                last = None
                for b0 in (0, 64):
                    last = q.matmul(PSA[sl][b0:b0+64, 128+b0:128+b0+64],
                                    xs[:, b0:b0+64], xu[:, b0:b0+64],
                                    start=True, stop=True)
                return last
            padd("tensor", t_raw, wait=[("ID", st["psa"])], inc="T",
                 sid=f"trw{u}")
            def t_r2(q, co=co, sl=sl):
                q.matmul(PSA[sl][:, 256:384], RNKL[0:2, co:co+128],
                         RNKM[0:2, co:co+128], start=True, stop=True)
                return q.matmul(PSA[sl][:, 384:512], RNKL[0:2, co:co+128],
                                RNKA[0:2, co:co+128], start=True, stop=True)
            padd("tensor", t_r2, wait=[("ID", f"grnk{h}")], inc="T",
                 sid=f"tr2{u}")
            def v_em(q, sl=sl):
                return q.tensor_add(EMS[sl][:], PSA[sl][:, 256:384], MASKS[:])
            padd("vector", v_em, wait=[("ID", f"tr2{u}")], inc="V",
                 sid=f"vem{u}")
            def a_em(q, sl=sl):
                return q.activation(EMS[sl][:], EMS[sl][:], AF.Exp)
            padd("scalar", a_em, wait=[("ID", f"vem{u}")], inc="A",
                 sid=f"aem{u}")
            def v_bt(q, sl=sl):
                return q.scalar_tensor_tensor(BT[sl][:], EMS[sl][:], -1.0,
                                              PSA[sl][:, 0:128], OP.mult,
                                              OP.mult)
            padd("vector", v_bt, wait=[("ID", f"aem{u}")], inc="V",
                 sid=f"vbt{u}")
            def v_ea(q, sl=sl):
                return q.tensor_add(EMS[sl][:], PSA[sl][:, 384:512], MASKI[:])
            padd("vector", v_ea, wait=(), inc="V", sid=f"vea{u}")
            def a_ea(q, sl=sl):
                return q.activation(EMS[sl][:], EMS[sl][:], AF.Exp)
            padd("scalar", a_ea, wait=[("ID", f"vea{u}")], inc="A",
                 sid=f"aea{u}")
            def v_att(q, sl=sl, cur=cur, co=co):
                return q.tensor_tensor(ATT[cur][:, co:co+128], EMS[sl][:],
                                       PSA[sl][:, 128:256], OP.mult)
            padd("vector", v_att, wait=[("ID", f"aea{u}")], inc="V",
                 sid=f"vat{u}")
            def v_pk(q, sl=sl):
                return q.tensor_add(PK[sl][:], BT[sl][:], IDENT[:])
            padd("vector", v_pk, wait=(), inc="V", sid=f"vpk{u}")
            pk_ring = [PK, PK2]
            def t_btt(q, sl=sl):
                return q.transpose(PTB[sl][:, 256:384], BT[sl][:], IDENT[:])
            padd("tensor", t_btt, wait=[("ID", f"vbt{u}"), ("ID", f"vvp{u}"),
                                        ("ID", st["ptb"])], inc="T",
                 sid=f"tbt{u}")
            def a_nkt0(q, sl=sl):
                return q.activation(NPA[sl][:, 0:128], PTB[sl][:, 256:384],
                                    AF.Copy)
            padd("scalar", a_nkt0, wait=[("ID", f"tbt{u}")], inc="A",
                 sid=f"vnk{u}")
            cur_nk = BT[sl][:]
            cur_nkt = NPA[sl][:, 0:128]
            ring = [NPB, NPA]
            for lvl in range(1, 6):
                if lvl == 1:
                    wsq = [("ID", f"vnk{u}"), ("ID", st["psb_a"]),
                           ("ID", st["psb_v"])]
                else:
                    wsq = [("ID", f"asq{u}_{lvl-1}"), ("ID", f"vpu{u}_{lvl-1}")]
                def t_sq(q, sl=sl, nk=cur_nk, nkt=cur_nkt, lvl=lvl):
                    last = q.matmul(PSB[sl][:, 0:128], nk, nkt,
                                    start=True, stop=True)
                    if lvl < 5:
                        last = q.matmul(PSB[sl][:, 128:256], nkt, nk,
                                        start=True, stop=True)
                    return last
                padd("tensor", t_sq, wait=wsq, inc="T", sid=f"tsq{u}_{lvl}")
                nxt = ring[lvl % 2]
                def a_sq(q, sl=sl, nxt=nxt, lvl=lvl):
                    if lvl < 5:
                        return q.activation(nxt[sl][:, 0:256], PSB[sl][:, 0:256],
                                            AF.Copy)
                    return q.activation(nxt[sl][:, 0:128], PSB[sl][:, 0:128],
                                        AF.Copy)
                padd("scalar", a_sq, wait=[("ID", f"tsq{u}_{lvl}")], inc="A",
                     sid=f"asq{u}_{lvl}")
                wpu = [("ID", f"asq{u}_{lvl}")]
                wpu.append(("ID", f"vpk{u}" if lvl == 1 else f"vpu{u}_{lvl-1}"))
                pk_src = pk_ring[(lvl - 1) % 2]
                pk_dst = pk_ring[lvl % 2]
                nxt_nkt = ring[lvl % 2][sl][:, 0:128]
                def t_pu(q, sl=sl, nkt2=nxt_nkt, pk_src=pk_src):
                    return q.matmul(PSB[sl][:, 256:384], nkt2,
                                    pk_src[sl][:], start=True, stop=True)
                padd("tensor", t_pu, wait=wpu, inc="T", sid=f"tpu{u}_{lvl}")
                def v_pu(q, sl=sl, pk_src=pk_src, pk_dst=pk_dst):
                    return q.tensor_add(pk_dst[sl][:], PSB[sl][:, 256:384],
                                        pk_src[sl][:])
                padd("vector", v_pu, wait=[("ID", f"tpu{u}_{lvl}")], inc="V",
                     sid=f"vpu{u}_{lvl}")
                cur_nk = ring[lvl % 2][sl][:, 128:256]
                cur_nkt = ring[lvl % 2][sl][:, 0:128]
            def t_ap(q, sl=sl, cur=cur, co=co):
                q.matmul(PSB[sl][:, 0:128], PK2[sl][:], VP[cur][:, co:co+128],
                         start=True, stop=True)
                return q.matmul(PSB[sl][:, 128:256], WBDT[sl][:], PK2[sl][:],
                                start=True, stop=True)
            padd("tensor", t_ap, wait=[("ID", f"vpu{u}_5"),
                                       ("ID", f"asq{u}_5")], inc="T",
                 sid=f"tap{u}")
            def a_ap(q, sl=sl, cur=cur, co=co):
                q.activation(VP[cur][:, co:co+128], PSB[sl][:, 0:128], AF.Copy)
                return q.activation(WKCD[cur][:, co:co+128],
                                    PSB[sl][:, 128:256], AF.Copy)
            padd("scalar", a_ap, wait=[("ID", f"tap{u}")], inc="A",
                 sid=f"aap{u}")
            st["ptb"] = f"vnk{u}"
            st["psa"] = f"vat{u}"
            st["psb_a"] = f"aap{u}"
            st["psb_v"] = f"vpu{u}_5"
            return ss

        for pp in range(0, 16, 2):
            sA = plan_pair(h, pp, 0)
            sB = plan_pair(h, pp + 1, 1)
            mix = []
            for i in range(max(len(sA), len(sB))):
                if i < len(sA):
                    mix.append(sA[i])
                if i < len(sB):
                    mix.append(sB[i])
            steps.extend(mix)
        headst["tr2last"] = f"tr2{h}_15"
        return steps

    def plan_head_scan(h, cur):
        steps = []
        def add(eng, fn, wait=(), inc=None, sid=None):
            steps.append(dict(eng=eng, fn=fn, wait=list(wait), inc=inc,
                              inc_n=1, self_inc=False, sid=sid))
        for n in range(NCH):
            p, half = n // 2, n % 2
            pb = 64 * half
            u = f"{h}_{n}"
            wvn = [("ID", f"aap{h}_15"), ("ID", f"vrk{h}_3"),
                   ("A", nA_edl)]
            if n > 0:
                wvn.append(("ID", f"as{h}_{n-1}"))
            elif headst["as_prev"]:
                wvn.append(("ID", headst["as_prev"]))
            def t_vn(q, h=h, n=n, pb=pb, cur=cur):
                return q.matmul(PSD[pb:pb+64, 0:128],
                                WKCD[cur][:, n*64:(n+1)*64],
                                SB16[:, h*128:(h+1)*128], start=True, stop=True)
            add("tensor", t_vn, wait=wvn, inc="T", sid=f"tvn{u}")
            def v_vn(q, n=n, p=p, pb=pb, cur=cur):
                return q.tensor_sub(VNEW[pb:pb+64, :],
                                    VP[cur][pb:pb+64, p*128:(p+1)*128],
                                    PSD[pb:pb+64, 0:128])
            add("vector", v_vn, wait=[("ID", f"tvn{u}")], inc="V",
                sid=f"vvn{u}")
            def t_o(q, h=h, n=n, p=p, pb=pb, cur=cur):
                q.matmul(PSD[:, 128:192], SB16[:, h*128:(h+1)*128],
                         RKD[cur][:, n*64:(n+1)*64], start=True, stop=False)
                return q.matmul(PSD[:, 128:192], VNEW[pb:pb+64, :],
                                ATT[cur][pb:pb+64, p*128+pb : p*128+pb+64],
                                start=False, stop=True)
            add("tensor", t_o, wait=[("ID", f"vvn{u}"), ("ID", f"vat{h}_{p}")],
                inc="T", sid=f"to{u}")
            def a_o(q, h=h, n=n):
                return q.activation(OF[:, h*T+n*64 : h*T+(n+1)*64],
                                    PSD[:, 128:192], AF.Copy)
            add("scalar", a_o, wait=[("ID", f"to{u}")], inc="A", sid=f"ao{u}")
            def t_s(q, n=n, p=p, pb=pb, cur=cur):
                return q.matmul(PSD[:, 192:320],
                                WKDW[cur][pb:pb+64, p*128:(p+1)*128],
                                VNEW[pb:pb+64, :], start=True, stop=True)
            add("tensor", t_s, wait=[("ID", f"vvn{u}"), ("ID", f"ao{u}")],
                inc="T", sid=f"ts{u}")
            def v_s(q, h=h, n=n):
                return q.scalar_tensor_tensor(SF32[:, h*128:(h+1)*128],
                                              SF32[:, h*128:(h+1)*128],
                                              EDL[:, h*32+n : h*32+n+1],
                                              PSD[:, 192:320], OP.mult, OP.add)
            add("vector", v_s, wait=[("ID", f"ts{u}"), ("ID", f"ao{u}")],
                inc="V", sid=f"vs{u}")
            def a_s(q, h=h):
                return q.activation(SB16[:, h*128:(h+1)*128],
                                    SF32[:, h*128:(h+1)*128], AF.Copy)
            add("scalar", a_s, wait=[("ID", f"vs{u}")], inc="A", sid=f"as{u}")
        headst["as_prev"] = f"as{h}_{NCH-1}"
        return steps

    def commit(steps):
        for st in steps:
            P.step(st["eng"], st["fn"], wait=st["wait"], inc=st["inc"],
                   inc_n=st["inc_n"], self_inc=st.get("self_inc", False),
                   sid=st.get("sid"))

    def merge(a, b):
        out = []
        la, lb = len(a), len(b)
        ia = ib = 0
        while ia < la or ib < lb:
            if ib >= lb or (ia < la and ia * lb <= ib * la):
                out.append(a[ia]); ia += 1
            else:
                out.append(b[ib]); ib += 1
        return out

    prev_scan = None
    if phase_limit >= 2:
        for h in range(8):
            cur = h % 2
            pairs = plan_head_pairs(h, cur)
            if prev_scan is not None:
                commit(merge(prev_scan, pairs))
            else:
                commit(pairs)
            prev_scan = plan_head_scan(h, cur)
        commit(prev_scan)

    def g_wo(q):
        last = None
        for kc in range(8):
            last = dmaG(q, WT[:, kc*1024:(kc+1)*1024], woT[kc*128:(kc+1)*128, :])
        return last
    nG_wo = P.step("gpsimd", g_wo, wait=[("ID", "tvp7_3")], inc="G",
                   inc_n=16 * 8, self_inc=True)

    if debug_dumps:
        def s_dump_p2(q):
            dmaO(q, dumps["d_of"], OF[:])
            dmaO(q, dumps["d_vp"], VP[1][:])
            dmaO(q, dumps["d_att"], ATT[1][:])
            return dmaO(q, dumps["d_wkcd"], WKCD[1][:])
        P.step("sync", s_dump_p2,
               wait=[("V", LAST), ("A", LAST), ("T", LAST)], inc="O", inc_n=64, self_inc=True)

    P.barrier(1)
    ps12.close()
    es12b.close()
    es12.close()

    es34 = ExitStack()
    X1B = es34.enter_context(nc.sbuf_tensor("X1B", [128, H * T], BF16, side="right"))
    es3 = ExitStack()
    XR2 = [sb(es3, f"XR{i}", [128, 8 * TT], F32) for i in range(2)]
    AO2 = [sb(es3, f"AO{i}", [128, 8 * TT], F32) for i in range(2)]
    AOB2 = [sb(es3, f"AOB{i}", [128, 8 * TT], BF16) for i in range(2)]
    SQ3 = [sb(es3, f"SQ3_{i}", [128, TT], BF16) for i in range(2)]
    MR = sb(es3, "MR", [1, TT], F32)
    VR = sb(es3, "VR", [1, TT], F32)
    RR = sb(es3, "RR", [1, TT], F32)
    ps3 = ExitStack()
    PAC = [psm(ps3, f"PAC{i}", [128, 512]) for i in range(2)]
    PSTm = psm(ps3, "PSTm", [1, 512])
    PSTe = psm(ps3, "PSTe", [1, 512])
    PBC = [psm(ps3, f"PBC{i}", [128, 512]) for i in range(2)]

    vao_bank = [0, 0]
    vao_tt = [("V", 0), ("V", 0)]
    vstats_prev = 0
    vx1_prev = 0
    tbc_prev = 0
    nA_x1_final = 0
    aaob_tt = [0, 0]
    tst_par3 = [0, 0]
    pending_apply = None

    def plan_ln1_apply(ptt, AO, nv_rcp):
        nonlocal vx1_prev, tbc_prev, nA_x1_final
        def t_bc(q):
            q.matmul(PBC[0][:, :], ONE128[:], MR[:], start=True, stop=True)
            return q.matmul(PBC[1][:, :], ONE128[:], RR[:], start=True,
                            stop=True)
        nTbc = P.step("tensor", t_bc, wait=[("V", nv_rcp), ("V", vx1_prev)],
                      inc="T")
        tbc_prev = nTbc
        for fc in range(8):
            def v_x1(q, fc=fc, AO=AO):
                q.tensor_sub(AO[:, fc*TT:(fc+1)*TT], AO[:, fc*TT:(fc+1)*TT],
                             PBC[0][:, :])
                return q.tensor_mul(AO[:, fc*TT:(fc+1)*TT],
                                    AO[:, fc*TT:(fc+1)*TT], PBC[1][:, :])
            nV = P.step("vector", v_x1, wait=[("T", nTbc)], inc="V")
            def s_x1(q, ptt=ptt, fc=fc, AO=AO):
                return dmaL(q, x1dram[fc*128:(fc+1)*128, ptt*TT:(ptt+1)*TT],
                           AO[:, fc*TT:(fc+1)*TT])
            P.step("sync", s_x1, wait=[("V", nV)], inc="L", inc_n=16,
                   self_inc=True)
            def a_x1b(q, ptt=ptt, fc=fc, AO=AO):
                return q.activation(X1B[:, fc*T+ptt*TT : fc*T+ptt*TT+TT],
                                    AO[:, fc*TT:(fc+1)*TT], AF.Copy)
            nA_x1_final = P.step("scalar", a_x1b, wait=[("V", nV)], inc="A")
        vx1_prev = P.cnt["V"]
        vao_tt[ptt % 2] = ("V", P.cnt["V"])
        aaob_tt[ptt % 2] = P.cnt["A"]

    for tt in (range(NTT) if phase_limit >= 3 else []):
        par = tt % 2
        XR = XR2[par]; AO = AO2[par]; AOB = AOB2[par]
        aaob_ids = [0] * 8
        def s_xr(q, tt=tt, XR=XR):
            return dmaL(q, XR.ap().rearrange("p (g t) -> p g t", g=8),
                       x_t[:, tt*TT:(tt+1)*TT].rearrange("(g p) t -> p g t", p=128))
        nLx = P.step("sync", s_xr, wait=[vao_tt[par], ("A", aaob_tt[par])],
                     inc="L", inc_n=16, self_inc=True)
        for fc in range(8):
            pk = fc % 2
            def t_wo(q, tt=tt, fc=fc, pk=pk):
                last = None
                for ec in range(8):
                    last = q.matmul(PAC[pk][:, :],
                                    WT[:, ec*1024+fc*128 : ec*1024+(fc+1)*128],
                                    OF[:, ec*T+tt*TT : ec*T+tt*TT+TT],
                                    start=(ec == 0), stop=(ec == 7))
                return last
            nT = P.step("tensor", t_wo,
                        wait=[("G", nG_wo), ("V", vao_bank[pk])], inc="T")
            def v_ao(q, fc=fc, pk=pk, XR=XR, AO=AO):
                return q.scalar_tensor_tensor(AO[:, fc*TT:(fc+1)*TT],
                                              XR[:, fc*TT:(fc+1)*TT], 2.0,
                                              PAC[pk][:, :], OP.mult, OP.add)
            nV = P.step("vector", v_ao, wait=[("T", nT), ("L", nLx)], inc="V")
            vao_bank[pk] = nV
            def a_aob(q, fc=fc, AO=AO, AOB=AOB):
                return q.activation(AOB[:, fc*TT:(fc+1)*TT],
                                    AO[:, fc*TT:(fc+1)*TT], AF.Copy)
            aaob_ids[fc] = P.step("scalar", a_aob, wait=[("V", nV)], inc="A")
        if pending_apply is not None:
            plan_ln1_apply(*pending_apply)
        for fc in range(8):
            def v_sq3(q, fc=fc, AOB=AOB):
                s = AOB[:, fc*TT:(fc+1)*TT]
                return q.tensor_tensor(SQ3[fc % 2][:], s, s, OP.mult)
            nV = P.step("vector", v_sq3,
                        wait=[("A", aaob_ids[fc]), ("T", tst_par3[fc % 2])],
                        inc="V")
            def t_st(q, fc=fc, AOB=AOB):
                q.matmul(PSTm[0:1, :], MEANC[:], AOB[:, fc*TT:(fc+1)*TT],
                         start=(fc == 0), stop=(fc == 7))
                return q.matmul(PSTe[0:1, :], MEANC[:], SQ3[fc % 2][:],
                                start=(fc == 0), stop=(fc == 7))
            tst_par3[fc % 2] = P.step("tensor", t_st,
                                      wait=[("V", nV)]
                                      + ([("V", vstats_prev)] if fc == 0
                                         else []), inc="T")
        nT_st = max(tst_par3)
        def v_stats(q):
            q.tensor_copy(MR[:], PSTm[0:1, :])
            q.tensor_mul(VR[:], MR[:], MR[:])
            q.tensor_sub(VR[:], PSTe[0:1, :], VR[:])
            return q.tensor_scalar_add(VR[:], VR[:], 1e-5)
        nV = P.step("vector", v_stats, wait=[("T", nT_st), ("T", tbc_prev)],
                    inc="V")
        vstats_prev = nV
        def a_sqrt(q):
            return q.activation(VR[:], VR[:], AF.Sqrt)
        nA = P.step("scalar", a_sqrt, wait=[("V", nV)], inc="A")
        def v_rcp(q):
            return q.reciprocal(RR[:], VR[:])
        nV = P.step("vector", v_rcp, wait=[("A", nA)], inc="V")
        pending_apply = (tt, AO, nV)
    if pending_apply is not None:
        plan_ln1_apply(*pending_apply)

    def s_scl(q):
        dmaL(q, SCL[0:1, 0:1], s0_h)
        dmaL(q, SCL[0:1, 1:2], s1_h)
        return dmaL(q, SCL[0:1, 2:3], s2_h)
    nLs = P.step("sync", s_scl, inc="L", inc_n=48, self_inc=True)
    def t_scl(q):
        return q.matmul(PBC[0][:, 0:3], ONE128[:], SCL[0:1, 0:3],
                        start=True, stop=True)
    nT = P.step("tensor", t_scl, wait=[("L", nLs), ("V", LAST)], inc="T")
    def v_scl(q):
        q.tensor_scalar_mul(SCL[:, 0:1], PBC[0][:, 0:1], 2.0)
        q.tensor_scalar_mul(SCL[:, 1:2], PBC[0][:, 1:2], 2.0)
        return q.tensor_scalar_mul(SCL[:, 2:3], PBC[0][:, 2:3], 2.0)
    P.step("vector", v_scl, wait=[("T", nT)], inc="V")

    if debug_dumps:
        def s_dump_p3(q):
            return dmaO(q, dumps["d_x1"], X1B[:])
        P.step("sync", s_dump_p3, wait=[("A", LAST), ("V", LAST)],
               inc="O", inc_n=16, self_inc=True)

    P.barrier(2)
    ps3.close()
    es3.close()
    es123.close()

    es4 = ExitStack()
    A0 = sb(es4, "A0", [128, 32 * TT], BF16)
    A1 = sb(es4, "A1", [128, 32 * TT], BF16)
    WS = [sb(es4, f"WS{i}", [128, 4096], BF16) for i in range(4)]
    KS = [sb(es4, f"KS{i}", [128, 4096], U8) for i in range(4)]
    X1R = sb(es4, "X1R", [128, 8 * TT], F32)
    Y4 = sb(es4, "Y4", [128, 8 * TT], F32)
    YB4 = sb(es4, "YB4", [128, 8 * TT], BF16)
    SQ4 = [sb(es4, f"SQ4_{i}", [128, TT], BF16) for i in range(2)]
    MR4 = sb(es4, "MR4", [1, TT], F32)
    VR4 = sb(es4, "VR4", [1, TT], F32)
    RR4 = sb(es4, "RR4", [1, TT], F32)
    ps4 = ExitStack()
    PAC4 = [psm(ps4, f"PAC4_{i}", [128, 512]) for i in range(2)]
    PST4m = psm(ps4, "PST4m", [1, 512])
    PST4e = psm(ps4, "PST4e", [1, 512])
    PBC4 = [psm(ps4, f"PBC4_{i}", [128, 512]) for i in range(2)]

    albank = [0, 0]
    vlbank = [0, 0]
    kwt_prev = [0, 0, 0, 0]
    awt_prev = [0, 0, 0, 0]
    tst_par_prev = [0, 0]
    vst4_prev = 0
    vx2_prev = 0
    ayb_final_prev = 0
    sout_prev = 0
    vl2_final_prev = 0
    tdone3_prev = [0, 0, 0, 0]
    l0_pre = None
    pending_apply4 = None

    def plan_ln2_apply(ptt, nv_rc4, ayb_all):
        nonlocal vx2_prev, sout_prev, ayb_final_prev
        def t_bc4(q):
            q.matmul(PBC4[0][:, :], ONE128[:], MR4[:], start=True, stop=True)
            return q.matmul(PBC4[1][:, :], ONE128[:], RR4[:], start=True,
                            stop=True)
        nTbc = P.step("tensor", t_bc4, wait=[("V", nv_rc4), ("V", vx2_prev)],
                      inc="T")
        ayb_final_prev = ayb_all
        for fc in range(8):
            def v_x2(q, fc=fc):
                q.tensor_sub(Y4[:, fc*TT:(fc+1)*TT], Y4[:, fc*TT:(fc+1)*TT],
                             PBC4[0][:, :])
                return q.tensor_mul(Y4[:, fc*TT:(fc+1)*TT],
                                    Y4[:, fc*TT:(fc+1)*TT], PBC4[1][:, :])
            nV = P.step("vector", v_x2,
                        wait=[("T", nTbc), ("A", ayb_all)], inc="V")
            def s_out(q, ptt=ptt, fc=fc):
                return dmaO(q, y_t[fc*128:(fc+1)*128, ptt*TT:(ptt+1)*TT],
                           Y4[:, fc*TT:(fc+1)*TT])
            P.step("sync", s_out, wait=[("V", nV)], inc="O", inc_n=16,
                   self_inc=True)
        vx2_prev = P.cnt["V"]
        sout_prev = P.cnt["O"]

    for tt in (range(NTT) if phase_limit >= 4 else []):
        def s_x1r(q, tt=tt):
            return dmaL(q, X1R.ap().rearrange("p (g t) -> p g t", g=8),
                       x1dram[:, tt*TT:(tt+1)*TT].rearrange("(g p) t -> p g t", p=128))
        nLr = P.step("sync", s_x1r, wait=[("V", vl2_final_prev)],
                     inc="L", inc_n=16, self_inc=True)

        def g_w0(q, f1c, buf):
            return dmaW(q, WS[buf][:, 0:2048], k0c_h[f1c], buf)
        tdone = list(tdone3_prev)
        if l0_pre is None:
            wsl = [0, 0, 0, 0]
            vd_pre = [0, 0, 0, 0]
            for j in range(3):
                wsl[j] = P.step("gpsimd", lambda q, j=j: g_w0(q, j, j),
                                wait=[("T", tdone[j])], inc=WSEM[j], inc_n=16,
                                self_inc=True)
                def v_dw0(q, buf=j):
                    return q.tensor_sub(WS[buf][:, 2048:3072],
                                        WS[buf][:, 0:1024],
                                        WS[buf][:, 1024:2048])
                vd_pre[j] = P.step("vector", v_dw0,
                                   wait=[(WSEM[j], wsl[j]), ("T", tdone[j])],
                                   inc="V")
        else:
            wsl, vd_pre = l0_pre
        for f1c in range(32):
            buf = f1c % 4
            nbuf = (f1c + 3) % 4
            if f1c + 3 < 32:
                wsl[nbuf] = P.step("gpsimd",
                                   lambda q, f1c=f1c, nbuf=nbuf: g_w0(q, f1c+3, nbuf),
                                   wait=[("T", tdone[nbuf])], inc=WSEM[nbuf],
                                   inc_n=16, self_inc=True)
            pk = f1c % 2
            if f1c < 3:
                nVd = vd_pre[f1c]
            else:
                def v_dw(q, buf=buf):
                    return q.tensor_sub(WS[buf][:, 2048:3072],
                                        WS[buf][:, 0:1024],
                                        WS[buf][:, 1024:2048])
                nVd = P.step("vector", v_dw,
                             wait=[(WSEM[buf], wsl[buf]), ("T", tdone[buf])],
                             inc="V")
            def t_l0(q, tt=tt, f1c=f1c, buf=buf, pk=pk):
                last = None
                for kc in range(8):
                    last = q.matmul(PAC4[pk][:, :],
                                    WS[buf][:, 2048+kc*128 : 2048+(kc+1)*128],
                                    X1B[:, kc*T+tt*TT : kc*T+tt*TT+TT],
                                    start=(kc == 0), stop=(kc == 7))
                return last
            nT = P.step("tensor", t_l0,
                        wait=[("V", nVd), ("A", albank[pk]),
                              ("A", nA_x1_final), ("V", vlbank[pk])], inc="T")
            tdone[buf] = nT
            def a_l0(q, tt=tt, f1c=f1c, pk=pk):
                return q.activation(A0[:, f1c*TT:(f1c+1)*TT], PAC4[pk][:, :],
                                    AF.Relu, scale=SCL[:, 0:1])
            albank[pk] = P.step("scalar", a_l0, wait=[("T", nT)], inc="A")
        nA_a0 = P.cnt["A"]

        if pending_apply4 is not None:
            plan_ln2_apply(*pending_apply4)
            pending_apply4 = None

        def g_w1(q, f2c, buf):
            return dmaW(q, KS[buf][:, :], k1_h[f2c], buf)
        wsl = [0, 0, 0, 0]
        kwt = list(kwt_prev)
        awt = list(awt_prev)
        for j in range(3):
            wsl[j] = P.step("gpsimd", lambda q, j=j: g_w1(q, j, j),
                            wait=[("V", kwt[j]), ("A", awt[j])], inc=WSEM[j],
                            inc_n=16, self_inc=True)
        for f2c in range(32):
            buf = f2c % 4
            nbuf = (f2c + 3) % 4
            if f2c + 3 < 32:
                wsl[nbuf] = P.step("gpsimd",
                                   lambda q, f2c=f2c, nbuf=nbuf: g_w1(q, f2c+3, nbuf),
                                   wait=[("V", kwt[nbuf]), ("A", awt[nbuf])],
                                   inc=WSEM[nbuf], inc_n=16, self_inc=True)
            pk = f2c % 2
            def v_wt(q, buf=buf):
                return q.tensor_scalar(WS[buf][:, 0:2048], KS[buf][:, 0:2048],
                                       2.0, -1.0, OP.mult, OP.add)
            kwt[buf] = P.step("vector", v_wt,
                              wait=[(WSEM[buf], wsl[buf]), ("T", tdone[buf])],
                              inc="V")
            def a_wt(q, buf=buf):
                return q.activation(WS[buf][:, 2048:4096],
                                    KS[buf][:, 2048:4096], AF.Copy,
                                    bias=-1.0, scale=2.0)
            awt[buf] = P.step("scalar", a_wt,
                              wait=[(WSEM[buf], wsl[buf]), ("T", tdone[buf])],
                              inc="A")
            def t_l1(q, tt=tt, f2c=f2c, buf=buf, pk=pk):
                last = None
                for kc in range(32):
                    last = q.matmul(PAC4[pk][:, :],
                                    WS[buf][:, kc*128:(kc+1)*128],
                                    A0[:, kc*TT:(kc+1)*TT], start=(kc == 0),
                                    stop=(kc == 31))
                return last
            nT = P.step("tensor", t_l1,
                        wait=[("V", kwt[buf]), ("A", awt[buf]),
                              ("A", albank[pk])]
                        + ([("A", nA_a0)] if f2c == 0 else []), inc="T")
            tdone[buf] = nT
            def a_l1(q, tt=tt, f2c=f2c, pk=pk):
                return q.activation(A1[:, f2c*TT:(f2c+1)*TT], PAC4[pk][:, :],
                                    AF.Relu, scale=SCL[:, 1:2])
            albank[pk] = P.step("scalar", a_l1, wait=[("T", nT)], inc="A")
        nA_a1 = P.cnt["A"]

        def g_w2(q, f3c, buf):
            return dmaW(q, KS[buf][:, :], k2_h[f3c], buf)
        wsl = [0, 0, 0, 0]
        for j in range(3):
            wsl[j] = P.step("gpsimd", lambda q, j=j: g_w2(q, j, j),
                            wait=[("V", kwt[j]), ("A", awt[j])], inc=WSEM[j],
                            inc_n=16, self_inc=True)
        tdone3 = list(tdone)
        ayb_ids = [0] * 8
        for f3c in range(8):
            buf = f3c % 4
            nbuf = (f3c + 3) % 4
            if f3c + 3 < 8:
                wsl[nbuf] = P.step("gpsimd",
                                   lambda q, f3c=f3c, nbuf=nbuf: g_w2(q, f3c+3, nbuf),
                                   wait=[("V", kwt[nbuf]), ("A", awt[nbuf])],
                                   inc=WSEM[nbuf], inc_n=16, self_inc=True)
            pk = f3c % 2
            def v_wt2(q, buf=buf):
                return q.tensor_scalar(WS[buf][:, 0:2048], KS[buf][:, 0:2048],
                                       2.0, -1.0, OP.mult, OP.add)
            kwt[buf] = P.step("vector", v_wt2,
                              wait=[(WSEM[buf], wsl[buf]), ("T", tdone3[buf])],
                              inc="V")
            def a_wt2(q, buf=buf):
                return q.activation(WS[buf][:, 2048:4096],
                                    KS[buf][:, 2048:4096], AF.Copy,
                                    bias=-1.0, scale=2.0)
            awt[buf] = P.step("scalar", a_wt2,
                              wait=[(WSEM[buf], wsl[buf]), ("T", tdone3[buf])],
                              inc="A")
            def t_l2(q, tt=tt, f3c=f3c, buf=buf, pk=pk):
                last = None
                for kc in range(32):
                    last = q.matmul(PAC4[pk][:, :],
                                    WS[buf][:, kc*128:(kc+1)*128],
                                    A1[:, kc*TT:(kc+1)*TT], start=(kc == 0),
                                    stop=(kc == 31))
                return last
            nT = P.step("tensor", t_l2,
                        wait=[("V", kwt[buf]), ("A", awt[buf]),
                              ("A", albank[pk]), ("V", vlbank[pk])]
                        + ([("A", nA_a1)] if f3c == 0 else []), inc="T")
            tdone3[buf] = nT
            def v_l2(q, tt=tt, f3c=f3c, pk=pk):
                q.tensor_scalar(Y4[:, f3c*TT:(f3c+1)*TT], PAC4[pk][:, :],
                                SCL[:, 2:3], None, OP.mult)
                return q.tensor_add(Y4[:, f3c*TT:(f3c+1)*TT],
                                    Y4[:, f3c*TT:(f3c+1)*TT],
                                    X1R[:, f3c*TT:(f3c+1)*TT])
            nV = P.step("vector", v_l2,
                        wait=[("T", nT), ("L", nLr), ("O", sout_prev),
                              ("A", ayb_final_prev)], inc="V")
            vlbank[pk] = nV
            def a_yb(q, f3c=f3c):
                return q.activation(YB4[:, f3c*TT:(f3c+1)*TT],
                                    Y4[:, f3c*TT:(f3c+1)*TT], AF.Copy)
            ayb_ids[f3c] = P.step("scalar", a_yb, wait=[("V", nV)], inc="A")
        vl2_final_prev = P.cnt["V"]

        if tt + 1 < NTT:
            pre_wsl = [0, 0, 0, 0]
            pre_vd = [0, 0, 0, 0]
            for j in range(3):
                pre_wsl[j] = P.step("gpsimd", lambda q, j=j: g_w0(q, j, j),
                                    wait=[("T", tdone3[j])], inc=WSEM[j],
                                    inc_n=16, self_inc=True)
                def v_dw_pre(q, buf=j):
                    return q.tensor_sub(WS[buf][:, 2048:3072],
                                        WS[buf][:, 0:1024],
                                        WS[buf][:, 1024:2048])
                pre_vd[j] = P.step("vector", v_dw_pre,
                                   wait=[(WSEM[j], pre_wsl[j]),
                                         ("T", tdone3[j])], inc="V")
            l0_pre = (pre_wsl, pre_vd)
        else:
            l0_pre = None

        tst_par = list(tst_par_prev)
        for fc in range(8):
            def v_sq4(q, fc=fc):
                s = YB4[:, fc*TT:(fc+1)*TT]
                return q.tensor_tensor(SQ4[fc % 2][:], s, s, OP.mult)
            nV = P.step("vector", v_sq4,
                        wait=[("A", ayb_ids[fc]), ("T", tst_par[fc % 2])],
                        inc="V")
            def t_st4(q, fc=fc):
                q.matmul(PST4m[0:1, :], MEANC[:], YB4[:, fc*TT:(fc+1)*TT],
                         start=(fc == 0), stop=(fc == 7))
                return q.matmul(PST4e[0:1, :], MEANC[:], SQ4[fc % 2][:],
                                start=(fc == 0), stop=(fc == 7))
            tst_par[fc % 2] = P.step("tensor", t_st4,
                                     wait=[("V", nV)]
                                     + ([("V", vst4_prev)] if fc == 0 else []),
                                     inc="T")
        nT_st = max(tst_par)
        tst_par_prev = tst_par
        def v_st4(q):
            q.tensor_copy(MR4[:], PST4m[0:1, :])
            q.tensor_mul(VR4[:], MR4[:], MR4[:])
            q.tensor_sub(VR4[:], PST4e[0:1, :], VR4[:])
            return q.tensor_scalar_add(VR4[:], VR4[:], 1e-5)
        nV = P.step("vector", v_st4, wait=[("T", nT_st)], inc="V")
        vst4_prev = nV
        def a_sq4(q):
            return q.activation(VR4[:], VR4[:], AF.Sqrt)
        nA = P.step("scalar", a_sq4, wait=[("V", nV)], inc="A")
        def v_rc4(q):
            return q.reciprocal(RR4[:], VR4[:])
        nV = P.step("vector", v_rc4, wait=[("A", nA)], inc="V")
        pending_apply4 = (tt, nV, P.cnt["A"])
        tdone3_prev = [max(tdone3[0], tdone[0]), max(tdone3[1], tdone[1]),
                       max(tdone3[2], tdone[2]), max(tdone3[3], tdone[3])]
        kwt_prev = list(kwt)
        awt_prev = list(awt)
    if pending_apply4 is not None:
        plan_ln2_apply(*pending_apply4)

    nO = P.cnt["O"]
    def v_fin(q, nO=nO):
        q.wait_ge(sems["O"], nO)
    P.prog["vector"].append(v_fin)

    with es0, es123, es34, es4, ps4, nc.Block() as block:
        @block.sync
        def _(q):
            for f in P.prog["sync"]:
                f(q)

        @block.gpsimd
        def _(q):
            for f in P.prog["gpsimd"]:
                f(q)

        @block.tensor
        def _(q):
            for f in P.prog["tensor"]:
                f(q)

        @block.vector
        def _(q):
            for f in P.prog["vector"]:
                f(q)

        @block.scalar
        def _(q):
            for f in P.prog["scalar"]:
                f(q)

    return nc


def _prep_shared(inputs):
    W_write = np.asarray(inputs["L0_attn_W_write"], np.float32)
    W_gate = np.asarray(inputs["L0_attn_W_gate"], np.float32)
    W_out = np.asarray(inputs["L0_attn_W_out"], np.float32)
    W_beta = np.asarray(inputs["L0_attn_W_beta"], np.float32)
    W_alpha = np.asarray(inputs["L0_attn_W_alpha"], np.float32)
    dtb = np.ascontiguousarray(np.asarray(inputs["L0_attn_dt_bias"],
                                          np.float32).reshape(8, 1))
    alog = np.ascontiguousarray(np.asarray(inputs["L0_attn_A_log"],
                                           np.float32).reshape(8, 1))
    k0 = np.asarray(inputs["L0_ffn_k0"]).astype(np.uint8)
    k1 = np.asarray(inputs["L0_ffn_k1"]).astype(np.uint8)
    k2 = np.asarray(inputs["L0_ffn_k2"]).astype(np.uint8)
    wgbaT = np.zeros((D, 104), np.float32)
    wgbaT[:, 0:8] = W_gate.T
    wgbaT[:, 32:40] = W_alpha.T
    wgbaT[:, 96:104] = W_beta.T
    shared = {
        "wwT": np.ascontiguousarray(W_write.T),
        "woT": np.ascontiguousarray(W_out.T),
        "wgbaT": np.ascontiguousarray(wgbaT),
        "dtb": dtb, "alog": alog,
        "k0c": np.ascontiguousarray(np.concatenate([
            k0[:D].reshape(8, 128, 32, 128).transpose(2, 1, 0, 3).reshape(32, 128, 1024),
            k0[D:].reshape(8, 128, 32, 128).transpose(2, 1, 0, 3).reshape(32, 128, 1024)],
            axis=2)),
        "k1": np.ascontiguousarray(
            k1.reshape(32, 128, 32, 128).transpose(2, 1, 0, 3).reshape(32, 128, 4096)),
        "k2": np.ascontiguousarray(
            k2.reshape(32, 128, 8, 128).transpose(2, 1, 0, 3).reshape(8, 128, 4096)),
        "s0": np.asarray(inputs["L0_ffn_s0"], np.float32).reshape(1, 1),
        "s1": np.asarray(inputs["L0_ffn_s1"], np.float32).reshape(1, 1),
        "s2": np.asarray(inputs["L0_ffn_s2"], np.float32).reshape(1, 1),
    }
    return shared


def kernel(**inputs):
    x = np.asarray(inputs["L0_attn_x"], np.float32)
    shared = _prep_shared(inputs)
    nc = build_nc(debug_dumps=False)
    in_maps = []
    for b in range(8):
        m = dict(shared)
        m["x_t"] = np.ascontiguousarray(x[b].T)
        in_maps.append(m)
    res = run_bass_kernel_spmd(nc, in_maps, core_ids=list(range(8)))
    out = np.empty((8, T, D), np.float32)
    for b in range(8):
        out[b] = res.results[b]["y_t"].T
    return out


# version 21
# speedup vs baseline: 1.1233x; 1.1233x over previous
import numpy as np
import ml_dtypes
from contextlib import ExitStack
import concourse.bass as bass
import concourse.mybir as mybir
from concourse.bass_utils import run_bass_kernel_spmd

F32 = mybir.dt.float32
BF16 = mybir.dt.bfloat16
U8 = mybir.dt.uint8
AF = mybir.ActivationFunctionType
OP = mybir.AluOpType

H, D, DH, C, T = 8, 1024, 128, 64, 2048
NCH = 32
TT = 512
NTT = T // TT
XBS = 2080

ENGS = ("sync", "gpsimd", "tensor", "vector", "scalar")
LAST = "LAST"


class Plan:

    def __init__(self, sems):
        self.prog = {e: [] for e in ENGS}
        self.sems = sems
        self.cnt = {k: 0 for k in sems}
        self.ids = {}
        self.waited = {e: {k: 0 for k in sems} for e in ENGS}

    def step(self, eng, fn, wait=(), inc=None, inc_n=1, self_inc=False,
             sid=None):
        bysem = {}
        for w in wait:
            if w[0] == "ID":
                if w[1] is None:
                    continue
                s, v = self.ids[w[1]]
            else:
                s, v = w
                if v == LAST:
                    v = self.cnt[s]
            bysem[s] = max(bysem.get(s, 0), v)
        todo = []
        for s, v in bysem.items():
            assert v <= self.cnt[s], f"wait on future {s}: {v} > {self.cnt[s]}"
            if v > self.waited[eng][s]:
                todo.append((s, v))
                self.waited[eng][s] = v
        sems = self.sems
        if inc is not None:
            self.cnt[inc] += inc_n
            if sid is not None:
                self.ids[sid] = (inc, self.cnt[inc])

        def run(q, todo=todo, fn=fn, inc=inc, inc_n=inc_n, self_inc=self_inc):
            for s, v in todo:
                q.wait_ge(sems[s], v)
            last = fn(q)
            if inc is not None and not self_inc:
                assert last is not None
                last.then_inc(sems[inc], inc_n)
        self.prog[eng].append(run)
        return self.cnt[inc] if inc is not None else None

    def barrier(self, k):
        sems = self.sems
        tgt = 5 * k
        self.cnt["B"] += 5
        assert self.cnt["B"] == tgt
        for e in ENGS:
            def f(q, tgt=tgt):
                q.sem_inc(sems["B"], 1)
                q.wait_ge(sems["B"], tgt)
            self.prog[e].append(f)
            self.waited[e]["B"] = tgt


def build_nc(debug_dumps=False, phase_limit=4):
    nc = bass.Bass("TRN2", target_bir_lowering=False, debug=False,
                   detect_race_conditions=False)

    x_t = nc.dram_tensor("x_t", [D, T], F32, kind="ExternalInput").ap()
    wwT = nc.dram_tensor("wwT", [D, D], F32, kind="ExternalInput").ap()
    woT = nc.dram_tensor("woT", [D, D], F32, kind="ExternalInput").ap()
    wgbaT = nc.dram_tensor("wgbaT", [D, 104], F32, kind="ExternalInput").ap()
    dtb_h = nc.dram_tensor("dtb", [8, 1], F32, kind="ExternalInput").ap()
    alog_h = nc.dram_tensor("alog", [8, 1], F32, kind="ExternalInput").ap()
    k0c_h = nc.dram_tensor("k0c", [32, 128, 2048], U8, kind="ExternalInput").ap()
    k1_h = nc.dram_tensor("k1", [32, 128, 4096], U8, kind="ExternalInput").ap()
    k2_h = nc.dram_tensor("k2", [8, 128, 4096], U8, kind="ExternalInput").ap()
    s0_h = nc.dram_tensor("s0", [1, 1], F32, kind="ExternalInput").ap()
    s1_h = nc.dram_tensor("s1", [1, 1], F32, kind="ExternalInput").ap()
    s2_h = nc.dram_tensor("s2", [1, 1], F32, kind="ExternalInput").ap()
    y_t = nc.dram_tensor("y_t", [D, T], F32, kind="ExternalOutput").ap()
    x1dram = nc.dram_tensor("x1dram", [D, T], F32, kind="Internal").ap()
    dumps = {}
    if debug_dumps:
        for nm, shape, dt in [("d_rowsp", [128, XBS], F32),
                              ("d_vti", [128, T], F32),
                              ("d_of", [128, H * T], BF16),
                              ("d_x1", [128, H * T], BF16),
                              ("d_vp", [128, T], BF16),
                              ("d_att", [128, T], BF16),
                              ("d_wkcd", [128, T], BF16),
                              ("d_edl", [128, 256], F32)]:
            dumps[nm] = nc.dram_tensor(nm, shape, dt, kind="ExternalOutput").ap()

    bf = ml_dtypes.bfloat16
    ident_c = nc.inline_tensor(np.eye(128, dtype=bf), name="ident_c").ap()
    idf32_c = nc.inline_tensor(np.eye(128, dtype=np.float32), name="idf32_c").ap()
    jj = np.arange(64)
    blkS = np.where(jj[None, :] > jj[:, None], 0.0, -1e30).astype(np.float32)
    blkI = np.where(jj[None, :] >= jj[:, None], 0.0, -1e30).astype(np.float32)
    mS = np.full((128, 128), -1e30, np.float32)
    mI = np.full((128, 128), -1e30, np.float32)
    for b0 in (0, 64):
        mS[b0:b0+64, b0:b0+64] = blkS
        mI[b0:b0+64, b0:b0+64] = blkI
    masks_c = nc.inline_tensor(mS, name="masks_c").ap()
    maski_c = nc.inline_tensor(mI, name="maski_c").ap()
    selc_np = np.zeros((128, 64), dtype=bf)
    for h in range(8):
        selc_np[:, h*8 + h] = bf(1.0)
    selc_c = nc.inline_tensor(selc_np, name="selc_c").ap()
    selr_np = np.zeros((128, 1024), np.float32)
    selr16_np = np.zeros((128, 1024), dtype=bf)
    for h in range(8):
        selr_np[32 + h, h*128:(h+1)*128] = 1.0
        selr16_np[64 + h, h*128:(h+1)*128] = bf(1.0)
    selr_c = nc.inline_tensor(selr_np, name="selr_c").ap()
    selr16_c = nc.inline_tensor(selr16_np, name="selr16_c").ap()
    meanc_c = nc.inline_tensor(np.full((128, 1), 1.0/1024, dtype=bf), name="meanc_c").ap()
    halfn_c = nc.inline_tensor(np.full((128, 1), -0.5, dtype=bf), name="halfn_c").ap()
    one128_c = nc.inline_tensor(np.ones((1, 128), np.float32), name="one128_c").ap()
    comb_np = np.zeros((4, 128, 128), np.float32)
    for h in range(8):
        comb_np[0, 0 + h, 64 + h] = 1.0
        comb_np[0, 32 + h, 0 + h] = 1.0
        comb_np[0, 96 + h, 32 + h] = 1.0
        comb_np[1, 32 + h, 64 + h] = 1.0
        comb_np[1, 32 + h, 32 + h] = 1.0
        comb_np[1, 32 + h, 96 + h] = -1.0
        comb_np[2, 0 + h, 64 + h] = 1.0
        comb_np[3, 0 + h, 32 + h] = 1.0
        comb_np[3, 0 + h, 0 + h] = 1.0
        comb_np[3, 0 + h, 96 + h] = 1.0
    comb_c = [nc.inline_tensor(comb_np[i], name=f"comb{i}_c").ap()
              for i in range(4)]
    onesr_c = nc.inline_tensor(np.ones((1, T), np.float32), name="onesr_c").ap()

    sem_names = ["L", "G", "H", "I", "J", "K", "T", "V", "A", "B", "O"]
    es0 = ExitStack()
    sems = {k: es0.enter_context(nc.semaphore("s" + k)) for k in sem_names}
    P = Plan(sems)

    sb = lambda st, name, shape, dt: st.enter_context(nc.sbuf_tensor(name, shape, dt))
    IDENT = sb(es0, "IDENT", [128, 128], BF16)
    IDF32 = sb(es0, "IDF32", [128, 128], F32)
    MASKS = sb(es0, "MASKS", [128, 128], F32)
    MASKI = sb(es0, "MASKI", [128, 128], F32)
    SELC = sb(es0, "SELC", [128, 64], BF16)
    SELR = sb(es0, "SELR", [128, 1024], F32)
    SELR16 = sb(es0, "SELR16", [128, 1024], BF16)
    MEANC = sb(es0, "MEANC", [128, 1], BF16)
    HALFN = sb(es0, "HALFN", [128, 1], BF16)
    ONE128 = sb(es0, "ONE128", [1, 128], F32)
    DTB = sb(es0, "DTB", [128, 1], F32)
    NEALOG = sb(es0, "NEALOG", [128, 1], F32)
    COMB = [sb(es0, f"COMB{i}", [128, 128], F32) for i in range(4)]
    SCL = sb(es0, "SCL", [128, 4], F32)

    es123 = ExitStack()
    WT = sb(es123, "WT", [128, 8 * 1024], BF16)
    OF = sb(es123, "OF", [128, H * T], BF16)

    es12 = ExitStack()
    XB = sb(es12, "XB", [128, 8 * XBS], BF16)
    WGBA = sb(es12, "WGBA", [128, 8 * 128], BF16)
    ROWSP = sb(es12, "ROWSP", [128, XBS], F32)
    RB16 = sb(es12, "RB16", [128, T], BF16)
    VT = sb(es12, "VT", [128, 16 * 128], F32)
    EDL = sb(es12, "EDL", [128, 256], F32)
    VF = [sb(es12, f"VF{i}", [128, T], BF16) for i in range(2)]
    VP = [sb(es12, f"VP{i}", [128, T], BF16) for i in range(2)]
    WKCD = [sb(es12, f"WKCD{i}", [128, T], BF16) for i in range(2)]
    ATT = [sb(es12, f"ATT{i}", [128, T], BF16) for i in range(2)]
    WKDW = [sb(es12, f"WKDW{i}", [128, T], BF16) for i in range(2)]
    RKD = [sb(es12, f"RKD{i}", [128, T], BF16) for i in range(2)]
    PK = [sb(es12, f"PK{i}", [128, 128], BF16) for i in range(2)]
    PK2 = [sb(es12, f"PK2{i}", [128, 128], BF16) for i in range(2)]
    NPA = [sb(es12, f"NPA{i}", [128, 256], BF16) for i in range(2)]
    NPB = [sb(es12, f"NPB{i}", [128, 256], BF16) for i in range(2)]
    EMS = [sb(es12, f"EMS{i}", [128, 128], F32) for i in range(2)]
    BT = [sb(es12, f"BT{i}", [128, 128], BF16) for i in range(2)]
    WBDT = [sb(es12, f"WBDT{i}", [128, 128], BF16) for i in range(2)]
    SF32 = sb(es12, "SF32", [128, H * 128], F32)
    SB16 = sb(es12, "SB16", [128, H * 128], BF16)
    VNEW = sb(es12, "VNEW", [128, 128], BF16)
    es12a = ExitStack()
    ROWST1 = sb(es12a, "ROWST1", [128, T], F32)
    ROWST2 = sb(es12a, "ROWST2", [128, T], F32)
    VTI = sb(es12a, "VTI", [128, T], F32)
    XSQ = sb(es12a, "XSQ", [128, TT], BF16)

    ps12 = ExitStack()
    psm = lambda st, name, shape, dt=F32: st.enter_context(nc.psum_tensor(name, shape, dt))
    PBIG = psm(ps12, "PBIG", [128, 512])
    PSA = [psm(ps12, f"PSA{i}", [128, 512]) for i in range(2)]
    PSB = [psm(ps12, f"PSB{i}", [128, 512]) for i in range(2)]
    PTB = [psm(ps12, f"PTB{i}", [128, 512], BF16) for i in range(2)]
    PSD = psm(ps12, "PSD", [128, 512])

    def dmaL(q, dst, src):
        return q.dma_start(out=dst, in_=src).then_inc(sems["L"], 16)

    def dmaG(q, dst, src):
        return q.dma_start(out=dst, in_=src).then_inc(sems["G"], 16)

    WSEM = ["H", "I", "J", "K"]

    def dmaW(q, dst, src, buf):
        return q.dma_start(out=dst, in_=src).then_inc(sems[WSEM[buf]], 16)

    def dmaO(q, dst, src):
        return q.dma_start(out=dst, in_=src).then_inc(sems["O"], 16)

    def c_loads(q):
        dmaL(q, IDENT[:], ident_c)
        dmaL(q, IDF32[:], idf32_c)
        dmaL(q, MASKS[:], masks_c)
        dmaL(q, MASKI[:], maski_c)
        dmaL(q, SELC[:], selc_c)
        dmaL(q, SELR[:], selr_c)
        dmaL(q, SELR16[:], selr16_c)
        dmaL(q, MEANC[:], meanc_c)
        dmaL(q, HALFN[:], halfn_c)
        dmaL(q, ONE128[:], one128_c)
        for i in range(4):
            dmaL(q, COMB[i][:], comb_c[i])
        dmaL(q, DTB[32:40, :], dtb_h)
        return dmaL(q, NEALOG[32:40, :], alog_h)
    if phase_limit >= 0.03:
        nL0 = P.step("sync", c_loads, inc="L", inc_n=16 * 16, self_inc=True)
    else:
        nL0 = 0

    def g_loads(q):
        last = None
        for h in range(8):
            last = dmaG(q, XB[:, h*XBS+32 : h*XBS+32+T], x_t[h*128:(h+1)*128, :])
        for kc in range(8):
            last = dmaG(q, WT[:, kc*1024:(kc+1)*1024], wwT[kc*128:(kc+1)*128, :])
        for kc in range(8):
            last = dmaG(q, WGBA[:, kc*128:kc*128+104], wgbaT[kc*128:(kc+1)*128, :])
        return last
    if phase_limit >= 0.12:
        nG0 = P.step("gpsimd", g_loads, inc="G", inc_n=16 * 24, self_inc=True)
    else:
        nG0 = 0

    def v_init(q):
        for h in range(8):
            q.memset(XB[:, h*XBS : h*XBS+32], 0.0)
        q.memset(ROWSP[:], 0.0)
        q.memset(ROWST1[:], 0.0)
        q.memset(ROWST2[:], 0.0)
        q.memset(SF32[:], 0.0)
        q.memset(VTI[:], 0.0)
        q.memset(PSA[0][:], 0.0)
        q.memset(PSA[1][:], 0.0)
        q.memset(PSD[:], 0.0)
        return q.memset(SB16[:], 0.0)
    if phase_limit >= 0.15:
        P.step("vector", v_init, wait=[("L", nL0), ("G", nG0)], inc="V")
    nV_init = P.cnt["V"]

    def a_nealog(q):
        q.activation(NEALOG[32:40, :], NEALOG[32:40, :], AF.Exp)
        return q.activation(NEALOG[32:40, :], NEALOG[32:40, :], AF.Copy,
                            scale=-1.0)
    nA_neal = (P.step("scalar", a_nealog, wait=[("L", nL0)], inc="A")
               if phase_limit >= 0.17 else 0)

    for tt in (range(NTT) if phase_limit >= 0.2 else []):
        def t_gba(q, tt=tt):
            last = None
            for kc in range(8):
                last = q.matmul(PSA[0][0:104, :], WGBA[:, kc*128:kc*128+104],
                                XB[:, kc*XBS+32+tt*TT : kc*XBS+32+tt*TT+TT],
                                start=(kc == 0), stop=(kc == 7))
            return last
        nT = P.step("tensor", t_gba,
                    wait=[("L", nL0), ("G", nG0), ("V", LAST)], inc="T")
        def v_gba(q, tt=tt):
            q.tensor_copy(ROWST1[0:8, tt*TT:(tt+1)*TT], PSA[0][0:8, :])
            q.tensor_copy(ROWST1[32:40, tt*TT:(tt+1)*TT], PSA[0][32:40, :])
            return q.tensor_copy(ROWST1[96:104, tt*TT:(tt+1)*TT], PSA[0][96:104, :])
        P.step("vector", v_gba, wait=[("T", nT)], inc="V")

    for tt in (range(NTT) if phase_limit >= 0.3 else []):
        for kc in range(8):
            def v_sq1(q, tt=tt, kc=kc):
                sl = XB[:, kc*XBS+32+tt*TT : kc*XBS+32+tt*TT+TT]
                return q.tensor_tensor(XSQ[:], sl, sl, OP.mult)
            nV = P.step("vector", v_sq1, wait=[("T", LAST)], inc="V")
            def t_ns(q, tt=tt, kc=kc):
                return q.matmul(PSA[1][0:8, :], SELC[:, kc*8:(kc+1)*8], XSQ[:],
                                start=(kc == 0), stop=(kc == 7))
            P.step("tensor", t_ns, wait=[("V", nV)], inc="T")
        def v_ns(q, tt=tt):
            return q.tensor_copy(ROWST2[0:8, tt*TT:(tt+1)*TT], PSA[1][0:8, :])
        P.step("vector", v_ns, wait=[("T", LAST)], inc="V")

    GL = ROWST1[0:8, 0:T]
    ZR = ROWST1[32:40, 0:T]
    BL = ROWST1[96:104, 0:T]
    NS2 = ROWST2[0:8, 0:T]
    DECS = ROWST2[32:40, 0:T]
    SCB = ROWST2[96:104, 0:T]
    LNNIV = ROWSP[0:8, 1:T+1]
    LNNSH = ROWSP[0:8, 0:T]
    T2G = ROWSP[0:8, 0:T]
    T2A = ROWSP[32:40, 0:T]
    T2B = ROWSP[96:104, 0:T]
    DWNIV = VTI[0:8, 0:T]
    T1G = VTI[0:8, 0:T]
    RA = VTI[32:40, 0:T]
    BETA = VTI[96:104, 0:T]
    EBA = RB16[64:72, 0:T]

    def a_rows1(q):
        q.activation(T1G, GL, AF.Abs)
        q.activation(T1G, T1G, AF.Exp, scale=-1.0)
        q.activation(T2G, T1G, AF.Ln, bias=1.0)
        return q.activation(T1G, GL, AF.Relu, scale=-1.0)
    nA = P.step("scalar", a_rows1, wait=[("V", LAST)], inc="A")
    def v_rows1(q):
        return q.scalar_tensor_tensor(GL, T1G, -1.0, T2G, OP.mult, OP.subtract)
    nV = P.step("vector", v_rows1, wait=[("A", nA)], inc="V")
    def a_rows2(q):
        q.activation(BETA, BL, AF.Sigmoid)
        q.activation(SCB, BL, AF.Abs)
        q.activation(SCB, SCB, AF.Exp, scale=-1.0)
        q.activation(T2B, SCB, AF.Ln, bias=1.0)
        return q.activation(SCB, BL, AF.Relu, scale=-1.0)
    nA = P.step("scalar", a_rows2, wait=[("V", nV)], inc="A")
    def v_rows2(q):
        return q.scalar_tensor_tensor(BL, SCB, -1.0, T2B, OP.mult, OP.subtract)
    nV = P.step("vector", v_rows2, wait=[("A", nA)], inc="V")
    def a_rows3(q):
        q.activation(DECS, ZR, AF.Abs, bias=DTB[32:40, :])
        q.activation(DECS, DECS, AF.Exp, scale=-1.0)
        q.activation(T2A, DECS, AF.Ln, bias=1.0)
        return q.activation(RA, ZR, AF.Relu, bias=DTB[32:40, :])
    nA = P.step("scalar", a_rows3, wait=[("V", nV)], inc="A")
    def v_rows3(q):
        q.tensor_add(ZR, RA, T2A)
        return q.tensor_scalar(DECS, ZR, NEALOG[32:40, :], None, OP.mult)
    nV = P.step("vector", v_rows3, wait=[("A", nA), ("A", nA_neal)], inc="V")

    src_is_dec = True
    for s in [1, 2, 4, 8, 16, 32]:
        def v_cs(q, s=s, src_is_dec=src_is_dec):
            sr = DECS if src_is_dec else RA
            ds = RA if src_is_dec else DECS
            s3 = sr.rearrange("p (n c) -> p n c", c=64)
            d3 = ds.rearrange("p (n c) -> p n c", c=64)
            q.tensor_copy(d3[:, :, 0:s], s3[:, :, 0:s])
            return q.tensor_add(d3[:, :, s:64], s3[:, :, s:64], s3[:, :, 0:64-s])
        nV = P.step("vector", v_cs, inc="V")
        src_is_dec = not src_is_dec
    assert src_is_dec

    def v_ln1(q):
        return q.tensor_scalar_max(LNNIV, NS2, 1e-24)
    nV = P.step("vector", v_ln1, inc="V")
    def a_ln(q):
        return q.activation(LNNIV, LNNIV, AF.Ln)
    nA = P.step("scalar", a_ln, wait=[("V", nV)], inc="A")
    def v_ln2(q):
        q.tensor_scalar_mul(LNNIV, LNNIV, -0.5)
        q.memset(ROWSP[0:8, 0:1], 0.0)
        dec3 = DECS.rearrange("p (n c) -> p n c", c=64)
        db3 = ZR.rearrange("p (n c) -> p n c", c=64)
        last3 = dec3[:, :, 63:64].broadcast_to([8, NCH, 64])
        return q.tensor_sub(db3[:, :, :], last3, dec3[:, :, :])
    nV = P.step("vector", v_ln2, wait=[("A", nA)], inc="V")

    nA_exps = None
    for tt in range(NTT):
        def t_cmb(q, tt=tt):
            c0, c1 = tt * TT, (tt + 1) * TT
            q.matmul(PSA[0][:, :], COMB[0][:], ROWST1[:, c0:c1],
                     start=True, stop=False)
            q.matmul(PSA[0][:, :], COMB[1][:], ROWST2[:, c0:c1],
                     start=False, stop=False)
            q.matmul(PSA[0][:, :], COMB[2][:], ROWSP[:, c0+1:c1+1],
                     start=False, stop=False)
            return q.matmul(PSA[0][:, :], COMB[3][:], ROWSP[:, c0:c1],
                            start=False, stop=True)
        nT = P.step("tensor", t_cmb, wait=[("V", nV), ("V", LAST), ("A", LAST)],
                    inc="T")
        def v_cmb(q, tt=tt):
            c0, c1 = tt * TT, (tt + 1) * TT
            q.tensor_copy(ROWSP[64:72, c0:c1], PSA[0][64:72, :])
            q.tensor_copy(ROWSP[32:40, c0:c1], PSA[0][32:40, :])
            return q.tensor_copy(ROWSP[96:104, c0:c1], PSA[0][96:104, :])
        nVc = P.step("vector", v_cmb, wait=[("T", nT)], inc="V")
        def a_cmb(q, tt=tt):
            c0, c1 = tt * TT, (tt + 1) * TT
            q.activation(RB16[64:72, c0:c1], PSA[0][64:72, :], AF.Exp)
            q.activation(VTI[32:40, c0:c1], PSA[0][32:40, :], AF.Exp)
            return q.activation(VTI[0:8, c0:c1], PSA[0][0:8, :], AF.Exp)
        nA_exps = P.step("scalar", a_cmb, wait=[("T", nT), ("V", nVc)], inc="A")

    def t_edl(q):
        dec3 = DECS.rearrange("p (n c) -> p n c", c=64)
        rhs = dec3[:, :, 63:64].rearrange("p n c -> p (n c)")
        last = None
        for h in range(8):
            last = q.matmul(PBIG[:, h*32:(h+1)*32], SELR[32:40, h*128:(h+1)*128],
                            rhs, start=True, stop=True)
        return last
    nT = P.step("tensor", t_edl, wait=[("V", nV)], inc="T")
    def a_edl(q):
        return q.activation(EDL[:, 0:256], PBIG[:, 0:256], AF.Exp)
    nA_edl = P.step("scalar", a_edl, wait=[("T", nT)], inc="A")

    for p in range(16):
        def t_vt(q, p=p):
            return q.transpose(PSB[0][:, 0:128], VTI[:, p*128:(p+1)*128],
                               IDF32[:])
        nT = P.step("tensor", t_vt, wait=[("A", nA_exps), ("V", LAST)], inc="T")
        def v_vt(q, p=p):
            return q.tensor_copy(VT[:, p*128:(p+1)*128], PSB[0][:, 0:128])
        P.step("vector", v_vt, wait=[("T", nT)], inc="V")

    if debug_dumps:
        def s_dump_rows(q):
            dmaO(q, dumps["d_rowsp"], ROWSP[:])
            dmaO(q, dumps["d_vti"], VTI[:])
            return dmaO(q, dumps["d_edl"], EDL[:])
        P.step("sync", s_dump_rows,
               wait=[("V", LAST), ("A", LAST)], inc="O", inc_n=48, self_inc=True)

    es12a.close()
    es12b = ExitStack()
    RNKL = sb(es12b, "RNKL", [2, T], F32)
    RNKM = sb(es12b, "RNKM", [2, T], F32)
    RNKA = sb(es12b, "RNKA", [2, T], F32)
    def s_rnk_ones(q):
        dmaL(q, RNKL[1:2, :], onesr_c)
        dmaL(q, RNKM[0:1, :], onesr_c)
        return dmaL(q, RNKA[0:1, :], onesr_c)
    nL_ones = P.step("sync", s_rnk_ones,
                     wait=[("V", LAST), ("A", LAST), ("T", LAST)],
                     inc="L", inc_n=48, self_inc=True)

    P.ids["z"] = ("V", 0)
    pst = [dict(ptb="z", psa="z", psb_a=None, psb_v="z"),
           dict(ptb="z", psa="z", psb_a=None, psb_v="z")]
    P.ids["p1_vvt"] = ("V", P.cnt["V"])
    pst[0]["psb_v"] = "p1_vvt"
    headst = {"pbig": None, "tr2last": None, "as_prev": None}
    P.ids["p1_aedl"] = ("A", P.cnt["A"])
    headst["pbig_a"] = "p1_aedl"

    def plan_head_pairs(h, cur):
        steps = []
        def add(eng, fn, wait=(), inc=None, sid=None):
            steps.append(dict(eng=eng, fn=fn, wait=list(wait), inc=inc,
                              inc_n=1, self_inc=False, sid=sid))
        hb = h * XBS

        for tt in range(NTT):
            def t_vp(q, h=h, tt=tt):
                last = None
                for kc in range(8):
                    last = q.matmul(PBIG[:, :],
                                    WT[:, kc*1024+h*128 : kc*1024+(h+1)*128],
                                    XB[:, kc*XBS+32+tt*TT : kc*XBS+32+tt*TT+TT],
                                    start=(kc == 0), stop=(kc == 7))
                return last
            add("tensor", t_vp, wait=[("ID", headst["pbig"]),
                                      ("ID", headst["pbig_a"])],
                inc="T", sid=f"tvp{h}_{tt}")
            def v_vf(q, tt=tt, cur=cur):
                return q.tensor_copy(VF[cur][:, tt*TT:(tt+1)*TT], PBIG[:, :])
            add("vector", v_vf, wait=[("ID", f"tvp{h}_{tt}")], inc="V",
                sid=f"vvf{h}_{tt}")
            headst["pbig"] = f"vvf{h}_{tt}"

        for tt in range(NTT):
            def t_rk(q, h=h, tt=tt):
                return q.matmul(PBIG[:, :], SELR16[64:72, h*128:(h+1)*128],
                                RB16[64:72, tt*TT:(tt+1)*TT], start=True,
                                stop=True)
            add("tensor", t_rk, wait=[("ID", headst["pbig"])], inc="T",
                sid=f"trk{h}_{tt}")
            def v_rk(q, h=h, tt=tt, cur=cur, hb=hb):
                return q.tensor_tensor(RKD[cur][:, tt*TT:(tt+1)*TT],
                                       XB[:, hb+32+tt*TT : hb+32+tt*TT+TT],
                                       PBIG[:, :], OP.mult)
            add("vector", v_rk, wait=[("ID", f"trk{h}_{tt}")], inc="V",
                sid=f"vrk{h}_{tt}")
            headst["pbig"] = f"vrk{h}_{tt}"

        def g_rnk(q, h=h):
            dmaG(q, RNKL[0:1, 0:T], ROWSP[96+h:96+h+1, 0:T])
            dmaG(q, RNKM[1:2, 0:T], ROWSP[32+h:32+h+1, 0:T])
            return dmaG(q, RNKA[1:2, 0:T], ROWSP[64+h:96+h+1-64+64-h+h, 0:T] if False else ROWSP[64+h:64+h+1, 0:T])
        add("gpsimd", g_rnk, wait=[("L", nL_ones), ("ID", headst["tr2last"]),
                                   ("V", P.cnt["V"]), ("A", P.cnt["A"])],
            inc="G", sid=f"grnk{h}")
        steps[-1]["inc_n"] = 48
        steps[-1]["self_inc"] = True

        def plan_pair(h, p, sl):
            st = pst[sl]
            u = f"{h}_{p}"
            ss = []
            def padd(eng, fn, wait=(), inc=None, sid=None):
                ss.append(dict(eng=eng, fn=fn, wait=list(wait), inc=inc,
                               inc_n=1, self_inc=False, sid=sid))
            co = p * 128
            xs = XB[:, hb + 31 + co : hb + 31 + co + 128]
            xu = XB[:, hb + 32 + co : hb + 32 + co + 128]

            def t_tx(q, xs=xs, sl=sl):
                return q.transpose(PTB[sl][:, 0:128], xs, IDENT[:])
            padd("tensor", t_tx, wait=[("ID", st["ptb"])], inc="T",
                 sid=f"ttx{u}")
            def a_wk(q, p=p, h=h, sl=sl, cur=cur, co=co):
                q.activation(WKDW[cur][:, co:co+128], PTB[sl][:, 0:128],
                             AF.Copy, scale=VT[:, p*128+0+h : p*128+0+h+1])
                return q.activation(WBDT[sl][:], PTB[sl][:, 0:128], AF.Copy,
                                    scale=VT[:, p*128+32+h : p*128+32+h+1])
            padd("scalar", a_wk, wait=[("ID", f"ttx{u}")], inc="A",
                 sid=f"vwk{u}")
            def t_tv(q, sl=sl, cur=cur, co=co):
                return q.transpose(PTB[sl][:, 128:256], VF[cur][:, co:co+128],
                                   IDENT[:])
            padd("tensor", t_tv, wait=[("ID", f"vwk{u}"),
                                       ("ID", f"vvf{h}_{p//4}")], inc="T",
                 sid=f"ttv{u}")
            def a_vp(q, p=p, h=h, sl=sl, cur=cur, co=co):
                return q.activation(VP[cur][:, co:co+128], PTB[sl][:, 128:256],
                                    AF.Copy,
                                    scale=VT[:, p*128+96+h : p*128+96+h+1])
            padd("scalar", a_vp, wait=[("ID", f"ttv{u}")], inc="A",
                 sid=f"vvp{u}")

            def t_raw(q, xs=xs, xu=xu, sl=sl):
                for b0 in (0, 64):
                    q.matmul(PSA[sl][b0:b0+64, b0:b0+64], xs[:, b0:b0+64],
                             xs[:, b0:b0+64], start=True, stop=True)
                last = None
                for b0 in (0, 64):
                    last = q.matmul(PSA[sl][b0:b0+64, 128+b0:128+b0+64],
                                    xs[:, b0:b0+64], xu[:, b0:b0+64],
                                    start=True, stop=True)
                return last
            padd("tensor", t_raw, wait=[("ID", st["psa"])], inc="T",
                 sid=f"trw{u}")
            def t_r2(q, co=co, sl=sl):
                q.matmul(PSA[sl][:, 256:384], RNKL[0:2, co:co+128],
                         RNKM[0:2, co:co+128], start=True, stop=True)
                return q.matmul(PSA[sl][:, 384:512], RNKL[0:2, co:co+128],
                                RNKA[0:2, co:co+128], start=True, stop=True)
            padd("tensor", t_r2, wait=[("ID", f"grnk{h}")], inc="T",
                 sid=f"tr2{u}")
            def v_em(q, sl=sl):
                return q.tensor_add(EMS[sl][:], PSA[sl][:, 256:384], MASKS[:])
            padd("vector", v_em, wait=[("ID", f"tr2{u}")], inc="V",
                 sid=f"vem{u}")
            def a_em(q, sl=sl):
                return q.activation(EMS[sl][:], EMS[sl][:], AF.Exp)
            padd("scalar", a_em, wait=[("ID", f"vem{u}")], inc="A",
                 sid=f"aem{u}")
            def v_bt(q, sl=sl):
                return q.scalar_tensor_tensor(BT[sl][:], EMS[sl][:], -1.0,
                                              PSA[sl][:, 0:128], OP.mult,
                                              OP.mult)
            padd("vector", v_bt, wait=[("ID", f"aem{u}")], inc="V",
                 sid=f"vbt{u}")
            def v_ea(q, sl=sl):
                return q.tensor_add(EMS[sl][:], PSA[sl][:, 384:512], MASKI[:])
            padd("vector", v_ea, wait=(), inc="V", sid=f"vea{u}")
            def a_ea(q, sl=sl):
                return q.activation(EMS[sl][:], EMS[sl][:], AF.Exp)
            padd("scalar", a_ea, wait=[("ID", f"vea{u}")], inc="A",
                 sid=f"aea{u}")
            def v_att(q, sl=sl, cur=cur, co=co):
                return q.tensor_tensor(ATT[cur][:, co:co+128], EMS[sl][:],
                                       PSA[sl][:, 128:256], OP.mult)
            padd("vector", v_att, wait=[("ID", f"aea{u}")], inc="V",
                 sid=f"vat{u}")
            def v_pk(q, sl=sl):
                return q.tensor_add(PK[sl][:], BT[sl][:], IDENT[:])
            padd("vector", v_pk, wait=(), inc="V", sid=f"vpk{u}")
            pk_ring = [PK, PK2]
            def t_btt(q, sl=sl):
                return q.transpose(PTB[sl][:, 256:384], BT[sl][:], IDENT[:])
            padd("tensor", t_btt, wait=[("ID", f"vbt{u}"), ("ID", f"vvp{u}"),
                                        ("ID", st["ptb"])], inc="T",
                 sid=f"tbt{u}")
            def a_nkt0(q, sl=sl):
                return q.activation(NPA[sl][:, 0:128], PTB[sl][:, 256:384],
                                    AF.Copy)
            padd("scalar", a_nkt0, wait=[("ID", f"tbt{u}")], inc="A",
                 sid=f"vnk{u}")
            cur_nk = BT[sl][:]
            cur_nkt = NPA[sl][:, 0:128]
            ring = [NPB, NPA]
            for lvl in range(1, 6):
                if lvl == 1:
                    wsq = [("ID", f"vnk{u}"), ("ID", st["psb_a"]),
                           ("ID", st["psb_v"])]
                else:
                    wsq = [("ID", f"asq{u}_{lvl-1}"), ("ID", f"vpu{u}_{lvl-1}")]
                def t_sq(q, sl=sl, nk=cur_nk, nkt=cur_nkt, lvl=lvl):
                    last = q.matmul(PSB[sl][:, 0:128], nk, nkt,
                                    start=True, stop=True)
                    if lvl < 5:
                        last = q.matmul(PSB[sl][:, 128:256], nkt, nk,
                                        start=True, stop=True)
                    return last
                padd("tensor", t_sq, wait=wsq, inc="T", sid=f"tsq{u}_{lvl}")
                nxt = ring[lvl % 2]
                def a_sq(q, sl=sl, nxt=nxt, lvl=lvl):
                    if lvl < 5:
                        return q.activation(nxt[sl][:, 0:256], PSB[sl][:, 0:256],
                                            AF.Copy)
                    return q.activation(nxt[sl][:, 0:128], PSB[sl][:, 0:128],
                                        AF.Copy)
                padd("scalar", a_sq, wait=[("ID", f"tsq{u}_{lvl}")], inc="A",
                     sid=f"asq{u}_{lvl}")
                wpu = [("ID", f"asq{u}_{lvl}")]
                wpu.append(("ID", f"vpk{u}" if lvl == 1 else f"vpu{u}_{lvl-1}"))
                pk_src = pk_ring[(lvl - 1) % 2]
                pk_dst = pk_ring[lvl % 2]
                nxt_nkt = ring[lvl % 2][sl][:, 0:128]
                def t_pu(q, sl=sl, nkt2=nxt_nkt, pk_src=pk_src):
                    return q.matmul(PSB[sl][:, 256:384], nkt2,
                                    pk_src[sl][:], start=True, stop=True)
                padd("tensor", t_pu, wait=wpu, inc="T", sid=f"tpu{u}_{lvl}")
                def v_pu(q, sl=sl, pk_src=pk_src, pk_dst=pk_dst):
                    return q.tensor_add(pk_dst[sl][:], PSB[sl][:, 256:384],
                                        pk_src[sl][:])
                padd("vector", v_pu, wait=[("ID", f"tpu{u}_{lvl}")], inc="V",
                     sid=f"vpu{u}_{lvl}")
                cur_nk = ring[lvl % 2][sl][:, 128:256]
                cur_nkt = ring[lvl % 2][sl][:, 0:128]
            def t_ap(q, sl=sl, cur=cur, co=co):
                q.matmul(PSB[sl][:, 0:128], PK2[sl][:], VP[cur][:, co:co+128],
                         start=True, stop=True)
                return q.matmul(PSB[sl][:, 128:256], WBDT[sl][:], PK2[sl][:],
                                start=True, stop=True)
            padd("tensor", t_ap, wait=[("ID", f"vpu{u}_5"),
                                       ("ID", f"asq{u}_5")], inc="T",
                 sid=f"tap{u}")
            def a_ap(q, sl=sl, cur=cur, co=co):
                q.activation(VP[cur][:, co:co+128], PSB[sl][:, 0:128], AF.Copy)
                return q.activation(WKCD[cur][:, co:co+128],
                                    PSB[sl][:, 128:256], AF.Copy)
            padd("scalar", a_ap, wait=[("ID", f"tap{u}")], inc="A",
                 sid=f"aap{u}")
            st["ptb"] = f"vnk{u}"
            st["psa"] = f"vat{u}"
            st["psb_a"] = f"aap{u}"
            st["psb_v"] = f"vpu{u}_5"
            return ss

        for pp in range(0, 16, 2):
            sA = plan_pair(h, pp, 0)
            sB = plan_pair(h, pp + 1, 1)
            mix = []
            for i in range(max(len(sA), len(sB))):
                if i < len(sA):
                    mix.append(sA[i])
                if i < len(sB):
                    mix.append(sB[i])
            steps.extend(mix)
        headst["tr2last"] = f"tr2{h}_15"
        return steps

    def plan_head_scan(h, cur):
        steps = []
        def add(eng, fn, wait=(), inc=None, sid=None):
            steps.append(dict(eng=eng, fn=fn, wait=list(wait), inc=inc,
                              inc_n=1, self_inc=False, sid=sid))
        for n in range(NCH):
            p, half = n // 2, n % 2
            pb = 64 * half
            u = f"{h}_{n}"
            wvn = [("ID", f"aap{h}_15"), ("ID", f"vrk{h}_3"),
                   ("A", nA_edl)]
            if n > 0:
                wvn.append(("ID", f"as{h}_{n-1}"))
            elif headst["as_prev"]:
                wvn.append(("ID", headst["as_prev"]))
            def t_vn(q, h=h, n=n, pb=pb, cur=cur):
                return q.matmul(PSD[pb:pb+64, 0:128],
                                WKCD[cur][:, n*64:(n+1)*64],
                                SB16[:, h*128:(h+1)*128], start=True, stop=True)
            add("tensor", t_vn, wait=wvn, inc="T", sid=f"tvn{u}")
            def v_vn(q, n=n, p=p, pb=pb, cur=cur):
                return q.tensor_sub(VNEW[pb:pb+64, :],
                                    VP[cur][pb:pb+64, p*128:(p+1)*128],
                                    PSD[pb:pb+64, 0:128])
            add("vector", v_vn, wait=[("ID", f"tvn{u}")], inc="V",
                sid=f"vvn{u}")
            def t_o(q, h=h, n=n, p=p, pb=pb, cur=cur):
                q.matmul(PSD[:, 128:192], SB16[:, h*128:(h+1)*128],
                         RKD[cur][:, n*64:(n+1)*64], start=True, stop=False)
                return q.matmul(PSD[:, 128:192], VNEW[pb:pb+64, :],
                                ATT[cur][pb:pb+64, p*128+pb : p*128+pb+64],
                                start=False, stop=True)
            add("tensor", t_o, wait=[("ID", f"vvn{u}"), ("ID", f"vat{h}_{p}")],
                inc="T", sid=f"to{u}")
            def a_o(q, h=h, n=n):
                return q.activation(OF[:, h*T+n*64 : h*T+(n+1)*64],
                                    PSD[:, 128:192], AF.Copy)
            add("scalar", a_o, wait=[("ID", f"to{u}")], inc="A", sid=f"ao{u}")
            def t_s(q, n=n, p=p, pb=pb, cur=cur):
                return q.matmul(PSD[:, 192:320],
                                WKDW[cur][pb:pb+64, p*128:(p+1)*128],
                                VNEW[pb:pb+64, :], start=True, stop=True)
            add("tensor", t_s, wait=[("ID", f"vvn{u}"), ("ID", f"ao{u}")],
                inc="T", sid=f"ts{u}")
            def v_s(q, h=h, n=n):
                return q.scalar_tensor_tensor(SF32[:, h*128:(h+1)*128],
                                              SF32[:, h*128:(h+1)*128],
                                              EDL[:, h*32+n : h*32+n+1],
                                              PSD[:, 192:320], OP.mult, OP.add)
            add("vector", v_s, wait=[("ID", f"ts{u}"), ("ID", f"ao{u}")],
                inc="V", sid=f"vs{u}")
            def a_s(q, h=h):
                return q.activation(SB16[:, h*128:(h+1)*128],
                                    SF32[:, h*128:(h+1)*128], AF.Copy)
            add("scalar", a_s, wait=[("ID", f"vs{u}")], inc="A", sid=f"as{u}")
        headst["as_prev"] = f"as{h}_{NCH-1}"
        return steps

    def commit(steps):
        for st in steps:
            P.step(st["eng"], st["fn"], wait=st["wait"], inc=st["inc"],
                   inc_n=st["inc_n"], self_inc=st.get("self_inc", False),
                   sid=st.get("sid"))

    def merge(a, b):
        out = []
        la, lb = len(a), len(b)
        ia = ib = 0
        while ia < la or ib < lb:
            if ib >= lb or (ia < la and ia * lb <= ib * la):
                out.append(a[ia]); ia += 1
            else:
                out.append(b[ib]); ib += 1
        return out

    prev_scan = None
    if phase_limit >= 2:
        for h in range(8):
            cur = h % 2
            pairs = plan_head_pairs(h, cur)
            if prev_scan is not None:
                commit(merge(prev_scan, pairs))
            else:
                commit(pairs)
            prev_scan = plan_head_scan(h, cur)
        commit(prev_scan)

    def g_wo(q):
        last = None
        for kc in range(8):
            last = dmaG(q, WT[:, kc*1024:(kc+1)*1024], woT[kc*128:(kc+1)*128, :])
        return last
    nG_wo = P.step("gpsimd", g_wo, wait=[("ID", "tvp7_3")], inc="G",
                   inc_n=16 * 8, self_inc=True)

    if debug_dumps:
        def s_dump_p2(q):
            dmaO(q, dumps["d_of"], OF[:])
            dmaO(q, dumps["d_vp"], VP[1][:])
            dmaO(q, dumps["d_att"], ATT[1][:])
            return dmaO(q, dumps["d_wkcd"], WKCD[1][:])
        P.step("sync", s_dump_p2,
               wait=[("V", LAST), ("A", LAST), ("T", LAST)], inc="O", inc_n=64, self_inc=True)

    P.barrier(1)
    ps12.close()
    es12b.close()
    es12.close()

    es34 = ExitStack()
    X1B = es34.enter_context(nc.sbuf_tensor("X1B", [128, H * T], BF16, side="right"))
    es3 = ExitStack()
    XR2 = [sb(es3, f"XR{i}", [128, 8 * TT], F32) for i in range(2)]
    AO2 = [sb(es3, f"AO{i}", [128, 8 * TT], F32) for i in range(2)]
    AOB2 = [sb(es3, f"AOB{i}", [128, 8 * TT], BF16) for i in range(2)]
    SQ3 = [sb(es3, f"SQ3_{i}", [128, TT], BF16) for i in range(2)]
    MR = sb(es3, "MR", [1, TT], F32)
    VR = sb(es3, "VR", [1, TT], F32)
    RR = sb(es3, "RR", [1, TT], F32)
    ps3 = ExitStack()
    PAC = [psm(ps3, f"PAC{i}", [128, 512]) for i in range(2)]
    PSTm = psm(ps3, "PSTm", [1, 512])
    PSTe = psm(ps3, "PSTe", [1, 512])
    PBC = [psm(ps3, f"PBC{i}", [128, 512]) for i in range(2)]

    vao_bank = [0, 0]
    vao_tt = [0, 0]
    vstats_prev = 0
    vx1_prev = 0
    tbc_prev = 0
    nA_x1_final = 0
    aaob_tt = [0, 0]
    tst_par3 = [0, 0]
    pending_apply = None

    def plan_ln1_apply(ptt, AO, nv_rcp):
        nonlocal vx1_prev, tbc_prev, nA_x1_final
        def t_bc(q):
            q.matmul(PBC[0][:, :], ONE128[:], MR[:], start=True, stop=True)
            return q.matmul(PBC[1][:, :], ONE128[:], RR[:], start=True,
                            stop=True)
        nTbc = P.step("tensor", t_bc, wait=[("V", nv_rcp), ("V", vx1_prev)],
                      inc="T")
        tbc_prev = nTbc
        for fc in range(8):
            def v_x1(q, fc=fc, AO=AO):
                q.tensor_sub(AO[:, fc*TT:(fc+1)*TT], AO[:, fc*TT:(fc+1)*TT],
                             PBC[0][:, :])
                return q.tensor_mul(AO[:, fc*TT:(fc+1)*TT],
                                    AO[:, fc*TT:(fc+1)*TT], PBC[1][:, :])
            nV = P.step("vector", v_x1, wait=[("T", nTbc)], inc="V")
            def s_x1(q, ptt=ptt, fc=fc, AO=AO):
                return dmaL(q, x1dram[fc*128:(fc+1)*128, ptt*TT:(ptt+1)*TT],
                           AO[:, fc*TT:(fc+1)*TT])
            P.step("sync", s_x1, wait=[("V", nV)], inc="L", inc_n=16,
                   self_inc=True)
            def a_x1b(q, ptt=ptt, fc=fc, AO=AO):
                return q.activation(X1B[:, fc*T+ptt*TT : fc*T+ptt*TT+TT],
                                    AO[:, fc*TT:(fc+1)*TT], AF.Copy)
            nA_x1_final = P.step("scalar", a_x1b, wait=[("V", nV)], inc="A")
        vx1_prev = P.cnt["V"]
        vao_tt[ptt % 2] = P.cnt["V"]
        aaob_tt[ptt % 2] = P.cnt["A"]

    for tt in (range(NTT) if phase_limit >= 3 else []):
        par = tt % 2
        XR = XR2[par]; AO = AO2[par]; AOB = AOB2[par]
        aaob_ids = [0] * 8
        def s_xr(q, tt=tt, XR=XR):
            return dmaL(q, XR.ap().rearrange("p (g t) -> p g t", g=8),
                       x_t[:, tt*TT:(tt+1)*TT].rearrange("(g p) t -> p g t", p=128))
        nLx = P.step("sync", s_xr, wait=[("V", vao_tt[par]), ("A", aaob_tt[par])],
                     inc="L", inc_n=16, self_inc=True)
        for fc in range(8):
            pk = fc % 2
            def t_wo(q, tt=tt, fc=fc, pk=pk):
                last = None
                for ec in range(8):
                    last = q.matmul(PAC[pk][:, :],
                                    WT[:, ec*1024+fc*128 : ec*1024+(fc+1)*128],
                                    OF[:, ec*T+tt*TT : ec*T+tt*TT+TT],
                                    start=(ec == 0), stop=(ec == 7))
                return last
            nT = P.step("tensor", t_wo,
                        wait=[("G", nG_wo), ("V", vao_bank[pk])], inc="T")
            def v_ao(q, fc=fc, pk=pk, XR=XR, AO=AO):
                return q.scalar_tensor_tensor(AO[:, fc*TT:(fc+1)*TT],
                                              XR[:, fc*TT:(fc+1)*TT], 2.0,
                                              PAC[pk][:, :], OP.mult, OP.add)
            nV = P.step("vector", v_ao, wait=[("T", nT), ("L", nLx)], inc="V")
            vao_bank[pk] = nV
            def a_aob(q, fc=fc, AO=AO, AOB=AOB):
                return q.activation(AOB[:, fc*TT:(fc+1)*TT],
                                    AO[:, fc*TT:(fc+1)*TT], AF.Copy)
            aaob_ids[fc] = P.step("scalar", a_aob, wait=[("V", nV)], inc="A")
        if pending_apply is not None:
            plan_ln1_apply(*pending_apply)
        for fc in range(8):
            def v_sq3(q, fc=fc, AOB=AOB):
                s = AOB[:, fc*TT:(fc+1)*TT]
                return q.tensor_tensor(SQ3[fc % 2][:], s, s, OP.mult)
            nV = P.step("vector", v_sq3,
                        wait=[("A", aaob_ids[fc]), ("T", tst_par3[fc % 2])],
                        inc="V")
            def t_st(q, fc=fc, AOB=AOB):
                q.matmul(PSTm[0:1, :], MEANC[:], AOB[:, fc*TT:(fc+1)*TT],
                         start=(fc == 0), stop=(fc == 7))
                return q.matmul(PSTe[0:1, :], MEANC[:], SQ3[fc % 2][:],
                                start=(fc == 0), stop=(fc == 7))
            tst_par3[fc % 2] = P.step("tensor", t_st,
                                      wait=[("V", nV)]
                                      + ([("V", vstats_prev)] if fc == 0
                                         else []), inc="T")
        nT_st = max(tst_par3)
        def v_stats(q):
            q.tensor_copy(MR[:], PSTm[0:1, :])
            q.tensor_mul(VR[:], MR[:], MR[:])
            q.tensor_sub(VR[:], PSTe[0:1, :], VR[:])
            return q.tensor_scalar_add(VR[:], VR[:], 1e-5)
        nV = P.step("vector", v_stats, wait=[("T", nT_st), ("T", tbc_prev)],
                    inc="V")
        vstats_prev = nV
        def a_sqrt(q):
            return q.activation(VR[:], VR[:], AF.Sqrt)
        nA = P.step("scalar", a_sqrt, wait=[("V", nV)], inc="A")
        def v_rcp(q):
            return q.reciprocal(RR[:], VR[:])
        nV = P.step("vector", v_rcp, wait=[("A", nA)], inc="V")
        pending_apply = (tt, AO, nV)
    if pending_apply is not None:
        plan_ln1_apply(*pending_apply)

    def s_scl(q):
        dmaL(q, SCL[0:1, 0:1], s0_h)
        dmaL(q, SCL[0:1, 1:2], s1_h)
        return dmaL(q, SCL[0:1, 2:3], s2_h)
    nLs = P.step("sync", s_scl, inc="L", inc_n=48, self_inc=True)
    def t_scl(q):
        return q.matmul(PBC[0][:, 0:3], ONE128[:], SCL[0:1, 0:3],
                        start=True, stop=True)
    nT = P.step("tensor", t_scl, wait=[("L", nLs), ("V", LAST)], inc="T")
    def v_scl(q):
        q.tensor_scalar_mul(SCL[:, 0:1], PBC[0][:, 0:1], 2.0)
        q.tensor_scalar_mul(SCL[:, 1:2], PBC[0][:, 1:2], 2.0)
        return q.tensor_scalar_mul(SCL[:, 2:3], PBC[0][:, 2:3], 2.0)
    P.step("vector", v_scl, wait=[("T", nT)], inc="V")

    if debug_dumps:
        def s_dump_p3(q):
            return dmaO(q, dumps["d_x1"], X1B[:])
        P.step("sync", s_dump_p3, wait=[("A", LAST), ("V", LAST)],
               inc="O", inc_n=16, self_inc=True)

    P.barrier(2)
    ps3.close()
    es3.close()
    es123.close()

    es4 = ExitStack()
    A0 = sb(es4, "A0", [128, 32 * TT], BF16)
    A1 = sb(es4, "A1", [128, 32 * TT], BF16)
    WS = [sb(es4, f"WS{i}", [128, 4096], BF16) for i in range(4)]
    KS = [sb(es4, f"KS{i}", [128, 4096], U8) for i in range(4)]
    X1R = sb(es4, "X1R", [128, 8 * TT], F32)
    Y4 = sb(es4, "Y4", [128, 8 * TT], F32)
    YB4 = sb(es4, "YB4", [128, 8 * TT], BF16)
    SQ4 = [sb(es4, f"SQ4_{i}", [128, TT], BF16) for i in range(2)]
    MR4 = sb(es4, "MR4", [1, TT], F32)
    VR4 = sb(es4, "VR4", [1, TT], F32)
    RR4 = sb(es4, "RR4", [1, TT], F32)
    ps4 = ExitStack()
    PAC4 = [psm(ps4, f"PAC4_{i}", [128, 512]) for i in range(2)]
    PST4m = psm(ps4, "PST4m", [1, 512])
    PST4e = psm(ps4, "PST4e", [1, 512])
    PBC4 = [psm(ps4, f"PBC4_{i}", [128, 512]) for i in range(2)]

    albank = [0, 0]
    vlbank = [0, 0]
    kwt_prev = [0, 0, 0, 0]
    tst_par_prev = [0, 0]
    vst4_prev = 0
    vx2_prev = 0
    ayb_final_prev = 0
    sout_prev = 0
    vl2_final_prev = 0
    tdone3_prev = [0, 0, 0, 0]
    l0_pre = None
    pending_apply4 = None

    def plan_ln2_apply(ptt, nv_rc4, ayb_all):
        nonlocal vx2_prev, sout_prev, ayb_final_prev
        def t_bc4(q):
            q.matmul(PBC4[0][:, :], ONE128[:], MR4[:], start=True, stop=True)
            return q.matmul(PBC4[1][:, :], ONE128[:], RR4[:], start=True,
                            stop=True)
        nTbc = P.step("tensor", t_bc4, wait=[("V", nv_rc4), ("V", vx2_prev)],
                      inc="T")
        ayb_final_prev = ayb_all
        for fc in range(8):
            def v_x2(q, fc=fc):
                q.tensor_sub(Y4[:, fc*TT:(fc+1)*TT], Y4[:, fc*TT:(fc+1)*TT],
                             PBC4[0][:, :])
                return q.tensor_mul(Y4[:, fc*TT:(fc+1)*TT],
                                    Y4[:, fc*TT:(fc+1)*TT], PBC4[1][:, :])
            nV = P.step("vector", v_x2,
                        wait=[("T", nTbc), ("A", ayb_all)], inc="V")
            def s_out(q, ptt=ptt, fc=fc):
                return dmaO(q, y_t[fc*128:(fc+1)*128, ptt*TT:(ptt+1)*TT],
                           Y4[:, fc*TT:(fc+1)*TT])
            P.step("sync", s_out, wait=[("V", nV)], inc="O", inc_n=16,
                   self_inc=True)
        vx2_prev = P.cnt["V"]
        sout_prev = P.cnt["O"]

    for tt in (range(NTT) if phase_limit >= 4 else []):
        def s_x1r(q, tt=tt):
            return dmaL(q, X1R.ap().rearrange("p (g t) -> p g t", g=8),
                       x1dram[:, tt*TT:(tt+1)*TT].rearrange("(g p) t -> p g t", p=128))
        nLr = P.step("sync", s_x1r, wait=[("V", vl2_final_prev)],
                     inc="L", inc_n=16, self_inc=True)

        def g_w0(q, f1c, buf):
            return dmaW(q, WS[buf][:, 0:2048], k0c_h[f1c], buf)
        tdone = list(tdone3_prev)
        a0_ids = []
        if l0_pre is None:
            wsl = [0, 0, 0, 0]
            vd_pre = [0, 0, 0, 0]
            for j in range(3):
                wsl[j] = P.step("gpsimd", lambda q, j=j: g_w0(q, j, j),
                                wait=[("T", tdone[j])], inc=WSEM[j], inc_n=16,
                                self_inc=True)
                def v_dw0(q, buf=j):
                    return q.tensor_sub(WS[buf][:, 2048:3072],
                                        WS[buf][:, 0:1024],
                                        WS[buf][:, 1024:2048])
                vd_pre[j] = P.step("vector", v_dw0,
                                   wait=[(WSEM[j], wsl[j]), ("T", tdone[j])],
                                   inc="V")
        else:
            wsl, vd_pre = l0_pre
        for f1c in range(32):
            buf = f1c % 4
            nbuf = (f1c + 3) % 4
            if f1c + 3 < 32:
                wsl[nbuf] = P.step("gpsimd",
                                   lambda q, f1c=f1c, nbuf=nbuf: g_w0(q, f1c+3, nbuf),
                                   wait=[("T", tdone[nbuf])], inc=WSEM[nbuf],
                                   inc_n=16, self_inc=True)
            pk = f1c % 2
            if f1c < 3:
                nVd = vd_pre[f1c]
            else:
                def v_dw(q, buf=buf):
                    return q.tensor_sub(WS[buf][:, 2048:3072],
                                        WS[buf][:, 0:1024],
                                        WS[buf][:, 1024:2048])
                nVd = P.step("vector", v_dw,
                             wait=[(WSEM[buf], wsl[buf]), ("T", tdone[buf])],
                             inc="V")
            def t_l0(q, tt=tt, f1c=f1c, buf=buf, pk=pk):
                last = None
                for kc in range(8):
                    last = q.matmul(PAC4[pk][:, :],
                                    WS[buf][:, 2048+kc*128 : 2048+(kc+1)*128],
                                    X1B[:, kc*T+tt*TT : kc*T+tt*TT+TT],
                                    start=(kc == 0), stop=(kc == 7))
                return last
            nT = P.step("tensor", t_l0,
                        wait=[("V", nVd), ("A", albank[pk]),
                              ("A", nA_x1_final), ("V", vlbank[pk])], inc="T")
            tdone[buf] = nT
            def a_l0(q, tt=tt, f1c=f1c, pk=pk):
                return q.activation(A0[:, f1c*TT:(f1c+1)*TT], PAC4[pk][:, :],
                                    AF.Relu, scale=SCL[:, 0:1])
            albank[pk] = P.step("scalar", a_l0, wait=[("T", nT)], inc="A")
            a0_ids.append(albank[pk])
        nA_a0 = P.cnt["A"]

        if pending_apply4 is not None:
            plan_ln2_apply(*pending_apply4)
            pending_apply4 = None

        def g_w1(q, f2c, buf):
            return dmaW(q, KS[buf][:, :], k1_h[f2c], buf)
        wsl = [0, 0, 0, 0]
        kwt = list(kwt_prev)
        a1_ids = []
        for j in range(3):
            wsl[j] = P.step("gpsimd", lambda q, j=j: g_w1(q, j, j),
                            wait=[("V", kwt[j])], inc=WSEM[j], inc_n=16,
                            self_inc=True)
        for f2c in range(32):
            buf = f2c % 4
            nbuf = (f2c + 3) % 4
            if f2c + 3 < 32:
                wsl[nbuf] = P.step("gpsimd",
                                   lambda q, f2c=f2c, nbuf=nbuf: g_w1(q, f2c+3, nbuf),
                                   wait=[("V", kwt[nbuf])], inc=WSEM[nbuf],
                                   inc_n=16, self_inc=True)
            pk = f2c % 2
            def v_wt(q, buf=buf):
                return q.tensor_scalar(WS[buf][:, :], KS[buf][:, :], 2.0, -1.0,
                                       OP.mult, OP.add)
            kwt[buf] = P.step("vector", v_wt,
                              wait=[(WSEM[buf], wsl[buf]), ("T", tdone[buf])],
                              inc="V")
            if f2c == 0:
                nT = 0
                for s4 in range(4):
                    def t_l1p(q, s4=s4, buf=buf, pk=pk):
                        last = None
                        for kc in range(s4*8, s4*8+8):
                            last = q.matmul(PAC4[pk][:, :],
                                            WS[buf][:, kc*128:(kc+1)*128],
                                            A0[:, kc*TT:(kc+1)*TT],
                                            start=(kc == 0), stop=(kc == 31))
                        return last
                    nT = P.step("tensor", t_l1p,
                                wait=[("V", kwt[buf]), ("A", albank[pk]),
                                      ("A", a0_ids[s4*8+7])], inc="T")
            else:
                def t_l1(q, tt=tt, f2c=f2c, buf=buf, pk=pk):
                    last = None
                    for kc in range(32):
                        last = q.matmul(PAC4[pk][:, :],
                                        WS[buf][:, kc*128:(kc+1)*128],
                                        A0[:, kc*TT:(kc+1)*TT],
                                        start=(kc == 0), stop=(kc == 31))
                    return last
                nT = P.step("tensor", t_l1,
                            wait=[("V", kwt[buf]), ("A", albank[pk])],
                            inc="T")
            tdone[buf] = nT
            def a_l1(q, tt=tt, f2c=f2c, pk=pk):
                return q.activation(A1[:, f2c*TT:(f2c+1)*TT], PAC4[pk][:, :],
                                    AF.Relu, scale=SCL[:, 1:2])
            albank[pk] = P.step("scalar", a_l1, wait=[("T", nT)], inc="A")
            a1_ids.append(albank[pk])
        nA_a1 = P.cnt["A"]

        def g_w2(q, f3c, buf):
            return dmaW(q, KS[buf][:, :], k2_h[f3c], buf)
        wsl = [0, 0, 0, 0]
        for j in range(3):
            wsl[j] = P.step("gpsimd", lambda q, j=j: g_w2(q, j, j),
                            wait=[("V", kwt[j])], inc=WSEM[j], inc_n=16,
                            self_inc=True)
        tdone3 = list(tdone)
        ayb_ids = [0] * 8
        for f3c in range(8):
            buf = f3c % 4
            nbuf = (f3c + 3) % 4
            if f3c + 3 < 8:
                wsl[nbuf] = P.step("gpsimd",
                                   lambda q, f3c=f3c, nbuf=nbuf: g_w2(q, f3c+3, nbuf),
                                   wait=[("V", kwt[nbuf])], inc=WSEM[nbuf],
                                   inc_n=16, self_inc=True)
            pk = f3c % 2
            def v_wt2(q, buf=buf):
                return q.tensor_scalar(WS[buf][:, :], KS[buf][:, :], 2.0, -1.0,
                                       OP.mult, OP.add)
            kwt[buf] = P.step("vector", v_wt2,
                              wait=[(WSEM[buf], wsl[buf]), ("T", tdone3[buf])],
                              inc="V")
            if f3c == 0:
                nT = 0
                for s4 in range(4):
                    def t_l2p(q, s4=s4, buf=buf, pk=pk):
                        last = None
                        for kc in range(s4*8, s4*8+8):
                            last = q.matmul(PAC4[pk][:, :],
                                            WS[buf][:, kc*128:(kc+1)*128],
                                            A1[:, kc*TT:(kc+1)*TT],
                                            start=(kc == 0), stop=(kc == 31))
                        return last
                    nT = P.step("tensor", t_l2p,
                                wait=[("V", kwt[buf]), ("A", albank[pk]),
                                      ("V", vlbank[pk]),
                                      ("A", a1_ids[s4*8+7])], inc="T")
            else:
                def t_l2(q, tt=tt, f3c=f3c, buf=buf, pk=pk):
                    last = None
                    for kc in range(32):
                        last = q.matmul(PAC4[pk][:, :],
                                        WS[buf][:, kc*128:(kc+1)*128],
                                        A1[:, kc*TT:(kc+1)*TT],
                                        start=(kc == 0), stop=(kc == 31))
                    return last
                nT = P.step("tensor", t_l2,
                            wait=[("V", kwt[buf]), ("A", albank[pk]),
                                  ("V", vlbank[pk])], inc="T")
            tdone3[buf] = nT
            def v_l2(q, tt=tt, f3c=f3c, pk=pk):
                q.tensor_scalar(Y4[:, f3c*TT:(f3c+1)*TT], PAC4[pk][:, :],
                                SCL[:, 2:3], None, OP.mult)
                return q.tensor_add(Y4[:, f3c*TT:(f3c+1)*TT],
                                    Y4[:, f3c*TT:(f3c+1)*TT],
                                    X1R[:, f3c*TT:(f3c+1)*TT])
            nV = P.step("vector", v_l2,
                        wait=[("T", nT), ("L", nLr), ("O", sout_prev),
                              ("A", ayb_final_prev)], inc="V")
            vlbank[pk] = nV
            def a_yb(q, f3c=f3c):
                return q.activation(YB4[:, f3c*TT:(f3c+1)*TT],
                                    Y4[:, f3c*TT:(f3c+1)*TT], AF.Copy)
            ayb_ids[f3c] = P.step("scalar", a_yb, wait=[("V", nV)], inc="A")
        vl2_final_prev = P.cnt["V"]

        if tt + 1 < NTT:
            pre_wsl = [0, 0, 0, 0]
            pre_vd = [0, 0, 0, 0]
            for j in range(3):
                pre_wsl[j] = P.step("gpsimd", lambda q, j=j: g_w0(q, j, j),
                                    wait=[("T", tdone3[j])], inc=WSEM[j],
                                    inc_n=16, self_inc=True)
                def v_dw_pre(q, buf=j):
                    return q.tensor_sub(WS[buf][:, 2048:3072],
                                        WS[buf][:, 0:1024],
                                        WS[buf][:, 1024:2048])
                pre_vd[j] = P.step("vector", v_dw_pre,
                                   wait=[(WSEM[j], pre_wsl[j]),
                                         ("T", tdone3[j])], inc="V")
            l0_pre = (pre_wsl, pre_vd)
        else:
            l0_pre = None

        tst_par = list(tst_par_prev)
        for fc in range(8):
            def v_sq4(q, fc=fc):
                s = YB4[:, fc*TT:(fc+1)*TT]
                return q.tensor_tensor(SQ4[fc % 2][:], s, s, OP.mult)
            nV = P.step("vector", v_sq4,
                        wait=[("A", ayb_ids[fc]), ("T", tst_par[fc % 2])],
                        inc="V")
            def t_st4(q, fc=fc):
                q.matmul(PST4m[0:1, :], MEANC[:], YB4[:, fc*TT:(fc+1)*TT],
                         start=(fc == 0), stop=(fc == 7))
                return q.matmul(PST4e[0:1, :], MEANC[:], SQ4[fc % 2][:],
                                start=(fc == 0), stop=(fc == 7))
            tst_par[fc % 2] = P.step("tensor", t_st4,
                                     wait=[("V", nV)]
                                     + ([("V", vst4_prev)] if fc == 0 else []),
                                     inc="T")
        nT_st = max(tst_par)
        tst_par_prev = tst_par
        def v_st4(q):
            q.tensor_copy(MR4[:], PST4m[0:1, :])
            q.tensor_mul(VR4[:], MR4[:], MR4[:])
            q.tensor_sub(VR4[:], PST4e[0:1, :], VR4[:])
            return q.tensor_scalar_add(VR4[:], VR4[:], 1e-5)
        nV = P.step("vector", v_st4, wait=[("T", nT_st)], inc="V")
        vst4_prev = nV
        def a_sq4(q):
            return q.activation(VR4[:], VR4[:], AF.Sqrt)
        nA = P.step("scalar", a_sq4, wait=[("V", nV)], inc="A")
        def v_rc4(q):
            return q.reciprocal(RR4[:], VR4[:])
        nV = P.step("vector", v_rc4, wait=[("A", nA)], inc="V")
        pending_apply4 = (tt, nV, P.cnt["A"])
        tdone3_prev = [max(tdone3[0], tdone[0]), max(tdone3[1], tdone[1]),
                       max(tdone3[2], tdone[2]), max(tdone3[3], tdone[3])]
        kwt_prev = list(kwt)
    if pending_apply4 is not None:
        plan_ln2_apply(*pending_apply4)

    nO = P.cnt["O"]
    def v_fin(q, nO=nO):
        q.wait_ge(sems["O"], nO)
    P.prog["vector"].append(v_fin)

    with es0, es123, es34, es4, ps4, nc.Block() as block:
        @block.sync
        def _(q):
            for f in P.prog["sync"]:
                f(q)

        @block.gpsimd
        def _(q):
            for f in P.prog["gpsimd"]:
                f(q)

        @block.tensor
        def _(q):
            for f in P.prog["tensor"]:
                f(q)

        @block.vector
        def _(q):
            for f in P.prog["vector"]:
                f(q)

        @block.scalar
        def _(q):
            for f in P.prog["scalar"]:
                f(q)

    return nc


def _prep_shared(inputs):
    W_write = np.asarray(inputs["L0_attn_W_write"], np.float32)
    W_gate = np.asarray(inputs["L0_attn_W_gate"], np.float32)
    W_out = np.asarray(inputs["L0_attn_W_out"], np.float32)
    W_beta = np.asarray(inputs["L0_attn_W_beta"], np.float32)
    W_alpha = np.asarray(inputs["L0_attn_W_alpha"], np.float32)
    dtb = np.ascontiguousarray(np.asarray(inputs["L0_attn_dt_bias"],
                                          np.float32).reshape(8, 1))
    alog = np.ascontiguousarray(np.asarray(inputs["L0_attn_A_log"],
                                           np.float32).reshape(8, 1))
    k0 = np.asarray(inputs["L0_ffn_k0"]).astype(np.uint8)
    k1 = np.asarray(inputs["L0_ffn_k1"]).astype(np.uint8)
    k2 = np.asarray(inputs["L0_ffn_k2"]).astype(np.uint8)
    wgbaT = np.zeros((D, 104), np.float32)
    wgbaT[:, 0:8] = W_gate.T
    wgbaT[:, 32:40] = W_alpha.T
    wgbaT[:, 96:104] = W_beta.T
    shared = {
        "wwT": np.ascontiguousarray(W_write.T),
        "woT": np.ascontiguousarray(W_out.T),
        "wgbaT": np.ascontiguousarray(wgbaT),
        "dtb": dtb, "alog": alog,
        "k0c": np.ascontiguousarray(np.concatenate([
            k0[:D].reshape(8, 128, 32, 128).transpose(2, 1, 0, 3).reshape(32, 128, 1024),
            k0[D:].reshape(8, 128, 32, 128).transpose(2, 1, 0, 3).reshape(32, 128, 1024)],
            axis=2)),
        "k1": np.ascontiguousarray(
            k1.reshape(32, 128, 32, 128).transpose(2, 1, 0, 3).reshape(32, 128, 4096)),
        "k2": np.ascontiguousarray(
            k2.reshape(32, 128, 8, 128).transpose(2, 1, 0, 3).reshape(8, 128, 4096)),
        "s0": np.asarray(inputs["L0_ffn_s0"], np.float32).reshape(1, 1),
        "s1": np.asarray(inputs["L0_ffn_s1"], np.float32).reshape(1, 1),
        "s2": np.asarray(inputs["L0_ffn_s2"], np.float32).reshape(1, 1),
    }
    return shared


def kernel(**inputs):
    x = np.asarray(inputs["L0_attn_x"], np.float32)
    shared = _prep_shared(inputs)
    nc = build_nc(debug_dumps=False)
    in_maps = []
    for b in range(8):
        m = dict(shared)
        m["x_t"] = np.ascontiguousarray(x[b].T)
        in_maps.append(m)
    res = run_bass_kernel_spmd(nc, in_maps, core_ids=list(range(8)))
    out = np.empty((8, T, D), np.float32)
    for b in range(8):
        out[b] = res.results[b]["y_t"].T
    return out
